# version 10
# baseline (speedup 1.0000x reference)
import contextlib
import numpy as np
import ml_dtypes
import concourse.bass as bass
import concourse.mybir as mybir
from concourse.bass_utils import run_bass_kernel_spmd
from concourse.alu_op_type import AluOpType as OP

F32 = mybir.dt.float32
BF16 = mybir.dt.bfloat16
F16 = mybir.dt.float16
I16 = mybir.dt.int16
I32 = mybir.dt.int32
AF = mybir.ActivationFunctionType
AX = mybir.AxisListType.X

P = 128
NCORES = 8
TFULL = 4096
C = 2048

ENGINES = ("sync", "gpsimd", "vector", "scalar", "tensor")


class Prog:

    def __init__(self, nc, stack):
        self.nc = nc
        self.stack = stack
        self.semstack = stack
        self.ops = {e: [] for e in ENGINES}
        self.sem = {e: stack.enter_context(nc.semaphore(f"sem_{e}")) for e in ENGINES}
        self.cnt = {e: 0 for e in ENGINES}
        self.satisfied = {e: {} for e in ENGINES}
        self.last_write = {}
        self.readers = {}
        self.uid = 0

    def sb(self, name, shape, dtype):
        self.uid += 1
        return self.stack.enter_context(self.nc.sbuf_tensor(f"{name}_u{self.uid}", shape, dtype))

    def ps(self, name, shape, dtype=F32):
        self.uid += 1
        return self.stack.enter_context(self.nc.psum_tensor(f"{name}_u{self.uid}", shape, dtype))

    def op(self, eng, fn, reads=(), writes=(), dma=None):
        deps = []
        for r in reads:
            lw = self.last_write.get(r)
            if lw is not None:
                deps.append(lw)
        for w in writes:
            lw = self.last_write.get(w)
            if lw is not None:
                deps.append(lw)
            deps.extend(self.readers.get(w, ()))
        bycnt = {}
        for sname, pcnt in deps:
            bycnt[sname] = max(bycnt.get(sname, 0), pcnt)
        waits = []
        for sname, pcnt in bycnt.items():
            sat = self.satisfied[eng].get(sname, 0)
            if pcnt > sat:
                self.satisfied[eng][sname] = pcnt
                waits.append((self.sem[sname], pcnt))
        if dma is not None:
            sname = "dma_" + dma
            if sname not in self.sem:
                self.sem[sname] = self.semstack.enter_context(self.nc.semaphore(sname))
                self.cnt[sname] = 0
            inc = 16
        else:
            sname = eng
            inc = 1
        self.cnt[sname] += inc
        mycnt = self.cnt[sname]
        sem_h = self.sem[sname]

        def emit(e, waits=waits, fn=fn, sem_h=sem_h, inc=inc):
            for h, v in waits:
                e.wait_ge(h, v)
            ins = fn(e)
            ins.then_inc(sem_h, inc)

        self.ops[eng].append(emit)
        tok = (sname, mycnt)
        for w in writes:
            self.last_write[w] = tok
            self.readers[w] = []
        for r in reads:
            self.readers.setdefault(r, []).append(tok)
        return tok

    def new_scope(self, stack):
        self.stack = stack
        self._consts = {}

    def constv(self, value, dtype=F32):
        key = (float(value), dtype)
        if not hasattr(self, "_consts"):
            self._consts = {}
        if key not in self._consts:
            self.uid += 1
            res = f"cpool{self.uid}"
            t = self.sb(res, [P, 1], dtype)
            self.op("vector", lambda v, t=t, value=value: v.memset(t[:], value), writes=[res])
            self._consts[key] = (t, res)
        t, res = self._consts[key]
        return t[:, 0:1], res

    def barrier(self):
        snap = dict(self.cnt)
        for eng in ENGINES:
            waits = []
            for peng, pcnt in snap.items():
                if pcnt <= 0 or peng == eng:
                    continue
                if self.satisfied[eng].get(peng, 0) < pcnt:
                    self.satisfied[eng][peng] = pcnt
                    waits.append((self.sem[peng], pcnt))
            if waits:
                def emit(e, waits=waits):
                    for h, v in waits:
                        e.wait_ge(h, v)
                self.ops[eng].append(emit)

    def finalize(self):
        waits = [(self.sem[s], self.cnt[s]) for s in self.cnt
                 if s.startswith("dma_") and self.cnt[s] > 0]
        def emit(e, waits=waits):
            for h, v in waits:
                e.wait_ge(h, v)
        self.ops["sync"].append(emit)

    def realize(self):
        with self.nc.Block() as block:
            for eng in ENGINES:
                ops = self.ops[eng]
                if not ops:
                    continue
                def runner(e, ops=ops):
                    for f in ops:
                        f(e)
                getattr(block, eng)(runner)



def emit_consts(p, d_ident):
    idb = p.sb("c_idb", [P, P], BF16)
    ones = p.sb("c_ones", [1, 512], BF16)
    p.op("gpsimd", lambda g: g.dma_start(idb[:], d_ident[:, :]), writes=["idb"], dma="idb")
    p.op("vector", lambda v: v.memset(ones[:], 1.0), writes=["ones"])
    return idb, ones


class LNTemps:
    def __init__(self, p, tag, D, shared_sq=None):
        self.D = D
        self.s1 = p.sb(f"ln_s1_{tag}", [P, 1], F32)
        self.negmean = p.sb(f"ln_nm_{tag}", [P, 1], F32)
        self.ss = p.sb(f"ln_ss_{tag}", [P, 1], F32)
        self.std = p.sb(f"ln_sd_{tag}", [P, 1], F32)
        self.rstd = p.sb(f"ln_rs_{tag}", [P, 1], F32)
        self.sq = shared_sq if shared_sq is not None else p.sb(f"ln_sq_{tag}", [P, D], BF16)
        self.res = f"lnt_{tag}"


def emit_ln_stats(p, xin_ap, rx, lt, eps=1e-5, extra_denom=None):
    D = lt.D
    r = lt.res
    p.op("vector", lambda v: v.reduce_sum(lt.s1[:], xin_ap, axis=AX), reads=[rx], writes=[r + "s1"])
    p.op("vector", lambda v: v.tensor_scalar_mul(lt.negmean[:], lt.s1[:], -1.0 / D),
         reads=[r + "s1"], writes=[r + "nm"])
    p.op("scalar", lambda s: s.activation(lt.sq[:], xin_ap, AF.Square, bias=lt.negmean[:, 0:1],
                                          accum_out=lt.ss[:]),
         reads=[rx, r + "nm"], writes=[r + "ss", r + "sq"])
    if extra_denom is None:
        beps, reps = p.constv(eps)
        p.op("scalar", lambda s: s.activation(lt.std[:], lt.ss[:], AF.Sqrt, scale=1.0 / D, bias=beps),
             reads=[r + "ss", reps], writes=[r + "std"])
    else:
        extra_denom(p, lt)
    p.op("vector", lambda v: v.reciprocal(lt.rstd[:], lt.std[:]), reads=[r + "std"], writes=[r + "rstd"])


def emit_ln_apply(p, xin_ap, rx, lt, out_ap, rout):
    p.op("vector", lambda v: v.tensor_scalar(out_ap, xin_ap, lt.negmean[:, 0:1], lt.rstd[:, 0:1],
                                             OP.add, OP.mult),
         reads=[rx, lt.res + "nm", lt.res + "rstd"], writes=[rout])


def emit_transpose_to_fm(p, src_ap_fn, nblk, pt, idb, dst, dst_col_fn, rsrc, rdst, dtile=128):
    for g0 in range(0, nblk, 4):
        gn = min(4, nblk - g0)
        def tfn(t, g0=g0, gn=gn):
            ins = None
            for j in range(gn):
                ins = t.transpose(pt[:, dtile * j:dtile * (j + 1)], src_ap_fn(g0 + j), idb[:])
            return ins
        p.op("tensor", tfn, reads=[rsrc, "idb"], writes=["pt"])
        def cfn(v, g0=g0, gn=gn):
            ins = None
            for j in range(gn):
                ins = v.tensor_copy(dst[:, dst_col_fn(g0 + j):dst_col_fn(g0 + j) + dtile],
                                    pt[:, dtile * j:dtile * (j + 1)])
            return ins
        p.op("vector", cfn, reads=["pt"], writes=[rdst])



def emit_f1(p, d_x, d_out, d_wgu, d_wd, d_ident, T=TFULL, TC=512):
    H = 8192
    nH = H // P
    nC = C // P
    ntile = TC // P
    nchunk = T // TC

    idb, ones = emit_consts(p, d_ident)

    xt = [p.sb(f"f1_xt{i}", [P, C], F32) for i in range(ntile)]
    lt = LNTemps(p, "f1", C)
    lt2 = LNTemps(p, "f1b", C)
    xn = p.sb("f1_xn", [P, C], BF16)
    xnT = p.sb("f1_xnT", [P, nC * TC], BF16)
    wbuf = [p.sb(f"f1_wb{i}", [P, 2 * nC * P], BF16) for i in range(2)]
    sg = [p.sb(f"f1_sg{i}", [P, TC], BF16) for i in range(2)]
    g64 = p.sb("f1_g64", [P, nH * TC], BF16)
    wdbuf = [p.sb(f"f1_wd{i}", [P, nH * P], BF16) for i in range(2)]
    yT = p.sb("f1_yT", [P, nC * TC], BF16)
    ynat = p.sb("f1_yn", [P, C], BF16)
    otile = p.sb("f1_ot", [P, C], F32)

    pt = p.ps("f1_pt", [P, 512], BF16)
    pg = [p.ps(f"f1_pg{i}", [P, TC]) for i in range(2)]
    pu = [p.ps(f"f1_pu{i}", [P, TC]) for i in range(2)]
    po = [p.ps(f"f1_po{i}", [P, TC]) for i in range(2)]

    for ch in range(nchunk):
        t0 = ch * ntile
        for t in range(ntile):
            row = (t0 + t) * P
            p.op("sync", lambda e, t=t, row=row: e.dma_start(xt[t][:], d_x[row:row + P, :]),
                 writes=[f"xt{t}"], dma=f"f1xt{t}")
        for t in range(ntile):
            emit_ln_stats(p, xt[t][:], f"xt{t}", lt)
            emit_ln_apply(p, xt[t][:], f"xt{t}", lt, xn[:], "xn")
            emit_transpose_to_fm(p, lambda b: xn[:, P * b:P * (b + 1)], nC, pt, idb,
                                 xnT, lambda b, t=t: TC * b + P * t, "xn", "xnT")
        for m in range(nH):
            par = m % 2
            p.op("gpsimd", lambda g, m=m, par=par: g.dma_start(wbuf[par][:], d_wgu[m, :, :]),
                 writes=[f"wb{par}"], dma=f"f1wb{par}")
            def mmg(t, par=par):
                ins = None
                for i in range(nC):
                    ins = t.matmul(pg[par][:], wbuf[par][:, P * i:P * (i + 1)],
                                   xnT[:, TC * i:TC * (i + 1)], start=(i == 0), stop=(i == nC - 1))
                return ins
            p.op("tensor", mmg, reads=[f"wb{par}", "xnT"], writes=[f"pg{par}"])
            p.op("scalar", lambda s, m=m, par=par: s.activation(sg[par][:], pg[par][:], AF.Silu),
                 reads=[f"pg{par}"], writes=[f"sg{par}"])
            def mmu(t, par=par):
                ins = None
                for i in range(nC):
                    ins = t.matmul(pu[par][:], wbuf[par][:, C + P * i:C + P * (i + 1)],
                                   xnT[:, TC * i:TC * (i + 1)], start=(i == 0), stop=(i == nC - 1))
                return ins
            p.op("tensor", mmu, reads=[f"wb{par}", "xnT"], writes=[f"pu{par}"])
            p.op("vector", lambda v, m=m, par=par: v.tensor_mul(g64[:, TC * m:TC * (m + 1)],
                                                                sg[par][:], pu[par][:]),
                 reads=[f"sg{par}", f"pu{par}"], writes=["g64"])
        for m in range(nC):
            par = m % 2
            p.op("gpsimd", lambda g, m=m, par=par: g.dma_start(wdbuf[par][:], d_wd[m, :, :]),
                 writes=[f"wd{par}"], dma=f"f1wd{par}")
            def mmd(t, par=par):
                ins = None
                for i in range(nH):
                    ins = t.matmul(po[par][:], wdbuf[par][:, P * i:P * (i + 1)],
                                   g64[:, TC * i:TC * (i + 1)], start=(i == 0), stop=(i == nH - 1))
                return ins
            p.op("tensor", mmd, reads=[f"wd{par}", "g64"], writes=[f"po{par}"])
            p.op("scalar", lambda s, m=m, par=par: s.activation(yT[:, TC * m:TC * (m + 1)],
                                                                po[par][:], AF.Copy),
                 reads=[f"po{par}"], writes=["yT"])
        for t in range(ntile):
            row = (t0 + t) * P
            emit_transpose_to_fm(p, lambda b, t=t: yT[:, TC * b + P * t:TC * b + P * (t + 1)],
                                 nC, pt, idb, ynat, lambda b: P * b, "yT", "ynat")
            emit_ln_stats(p, ynat[:], "ynat", lt2)
            emit_ln_apply(p, ynat[:], "ynat", lt2, otile[:], "otile")
            p.op("vector", lambda v, t=t: v.tensor_add(otile[:], otile[:], xt[t][:]),
                 reads=["otile", f"xt{t}"], writes=["otile"])
            p.op("sync", lambda e, row=row: e.dma_start(d_out[row:row + P, :], otile[:]),
                 reads=["otile"], writes=["otile_dma"], dma="ostore")


def prep_f1_weights(w_gate, w_up, w_down):
    def prep_gu(w):
        wt = np.ascontiguousarray(w.T)
        a = wt.reshape(16, 128, 64, 128)
        return np.ascontiguousarray(a.transpose(2, 1, 0, 3)).reshape(64, 128, 2048)
    def prep_d(w):
        wt = np.ascontiguousarray(w.T)
        a = wt.reshape(64, 128, 16, 128)
        return np.ascontiguousarray(a.transpose(2, 1, 0, 3)).reshape(16, 128, 8192)
    wgu = np.ascontiguousarray(np.concatenate([prep_gu(w_gate), prep_gu(w_up)], axis=2))
    return wgu, prep_d(w_down)


IDENT = np.eye(128, dtype=np.float32)


def build_f1(T=TFULL, TC=512):
    nc = bass.Bass("TRN2", target_bir_lowering=False)
    d_x = nc.dram_tensor("x", [T, C], F32, kind="ExternalInput")
    d_wg = nc.dram_tensor("wg", [64, 128, 4096], F32, kind="ExternalInput")
    d_wd = nc.dram_tensor("wd", [16, 128, 8192], F32, kind="ExternalInput")
    d_id = nc.dram_tensor("ident", [128, 128], F32, kind="ExternalInput")
    d_out = nc.dram_tensor("out", [T, C], F32, kind="ExternalOutput")
    with contextlib.ExitStack() as stack:
        p = Prog(nc, stack)
        emit_f1(p, d_x, d_out, d_wg, d_wd, d_id, T=T, TC=TC)
        p.finalize()
        p.realize()
    return nc



def emit_a1(p, d_x, d_out, d_wq, d_wp, d_cosT, d_sinT, d_mdiag, d_manti, d_ppair_in,
            d_ident, T=TFULL, TQ=512):
    HQ, HKV, DH = 16, 4, 128
    nC = C // P
    ntile = TQ // P
    nq = T // TQ
    RING = 8

    idb, ones = emit_consts(p, d_ident)
    ones128 = p.sb("a1_ones128", [P, P], BF16)
    p.op("vector", lambda v: v.memset(ones128[:], 1.0), writes=["ones128"])
    ppair = p.sb("a1_pp", [P, P], BF16)
    mdiag = p.sb("a1_md", [P, P], BF16)
    manti = p.sb("a1_ma", [P, P], BF16)
    p.op("gpsimd", lambda g: g.dma_start(ppair[:], d_ppair_in[:, :]), writes=["ppair"], dma="a1pp")
    p.op("gpsimd", lambda g: g.dma_start(mdiag[:], d_mdiag[:, :]), writes=["mdiag"], dma="mdiag")
    p.op("gpsimd", lambda g: g.dma_start(manti[:], d_manti[:, :]), writes=["manti"], dma="manti")

    xt = [p.sb(f"a1_xt{i}", [P, C], F32) for i in range(2)]
    xres = p.sb("a1_xr", [P, C], F32)
    lt = LNTemps(p, "a1", C)
    lt2 = LNTemps(p, "a1b", C, shared_sq=lt.sq)
    xn = p.sb("a1_xn", [P, C], BF16)
    xnT = p.sb("a1_xnT", [P, nC * TQ], BF16)
    ynat = p.sb("a1_ynat", [P, ntile * C], BF16)
    qkvT = [p.sb(f"a1_qkv{i}", [P, 20 * TQ], BF16) for i in range(2)]
    vtmp = [p.sb("a1_vt", [P, TQ], BF16)] * 2
    Kring = p.sb("a1_KR", [P, HKV * RING * P], BF16)
    Vring = p.sb("a1_VR", [P, RING * HKV * P], BF16)
    yT = p.sb("a1_yT", [P, HQ * TQ], BF16)
    wqb = [p.sb(f"a1_wq{i}", [P, 4096], BF16) for i in range(3)]
    wpb = p.sb("a1_wp", [P, 16 * 512], BF16)
    cosb = p.sb("a1_cos", [P, TQ], F32)
    sinb = p.sb("a1_sin", [P, TQ], F32)
    rt1 = p.sb("a1_rt1", [P, 512], F32)
    sqb = [p.sb(f"a1_sq{i}", [P, 512], BF16) for i in range(2)]
    sstd = [p.sb(f"a1_sstd{i}", [P, 512], BF16) for i in range(2)]
    rstd = [p.sb(f"a1_rstd{i}", [P, 512], F32) for i in range(2)]
    ET = [p.sb(f"a1_ET{i}", [P, 512], BF16) for i in range(2)]
    rzf = [p.sb(f"a1_rzf{i}", [P, 512], F32) for i in range(2)]
    otile = p.sb("a1_ot", [P, C // 2], F32)

    pt = p.ps("a1_pt", [P, 1024], BF16)
    pq = [p.ps(f"a1_pq{i}", [P, 512]) for i in range(2)]
    pzb = [p.ps(f"a1_pz{i}", [P, 512]) for i in range(2)]
    yAV = [p.ps(f"a1_yav{i}", [P, 512]) for i in range(2)]

    beps_q, r_beps_q = p.constv(128.0 * 1.1920929e-07)
    beps_k, r_beps_k = p.constv(1.1920929e-07)

    def ph1(qq):
        for t in range(ntile):
            row = (qq * ntile + t) * P
            xp = xt[t % 2]
            p.op("sync", lambda e, row=row, xp=xp: e.dma_start(xp[:], d_x[row:row + P, :]),
                 writes=["xt0"], dma="a1xt0")
            emit_ln_stats(p, xp[:], "xt0", lt)
            emit_ln_apply(p, xp[:], "xt0", lt, xn[:], "xn")
            emit_transpose_to_fm(p, lambda b: xn[:, P * b:P * (b + 1)], nC, pt, idb,
                                 xnT, lambda b, t=t: TQ * b + P * t, "xn", "xnT")

    ph1(0)
    for q in range(nq):
        t0 = q * ntile
        qpar = q % 2
        qkv = qkvT[qpar]
        rqkv = f"qkv{qpar}"
        s0 = (4 * q) % RING
        p.op("sync", lambda e, q=q: e.dma_start(cosb[:], d_cosT[:, TQ * q:TQ * (q + 1)]),
             writes=["cosb"], dma="a1cos")
        p.op("sync", lambda e, q=q: e.dma_start(sinb[:], d_sinT[:, TQ * q:TQ * (q + 1)]),
             writes=["sinb"], dma="a1sin")

        def load_plane(pl):
            wb = wqb[pl % 3]
            p.op("gpsimd", lambda g_, pl=pl, wb=wb: g_.dma_start(wb[:], d_wq[pl, :, :]),
                 writes=[f"wq{pl % 3}"], dma=f"a1wq{pl % 3}")
        for pl in range(3):
            load_plane(pl)
        for n in range(6):
            for j in range(4):
                o = 4 * n + j
                par = o % 2
                pp = pq[par]
                def mq(tt, n=n, j=j, pp=pp):
                    ins = None
                    for i in range(nC):
                        wb = wqb[(2 * n + i // 8) % 3]
                        ins = tt.matmul(pp[:], wb[:, 512 * (i % 8) + 128 * j:512 * (i % 8) + 128 * (j + 1)],
                                        xnT[:, TQ * i:TQ * (i + 1)], start=(i == 0), stop=(i == nC - 1))
                    return ins
                p.op("tensor", mq, reads=[f"wq{(2 * n) % 3}", f"wq{(2 * n + 1) % 3}", "xnT"],
                     writes=[f"pq{par}"])
                if o < 20:
                    tile = qkv[:, 512 * o:512 * (o + 1)]
                    rtile = rqkv
                else:
                    tile = vtmp[par][:]
                    rtile = "vt"
                p.op("scalar", lambda s, tile=tile, pp=pp: s.activation(tile, pp[:], AF.Copy),
                     reads=[f"pq{par}"], writes=[rtile])
                if o < 20:
                    p.op("tensor", lambda t_, tile=tile, pp=pp: t_.matmul(pp[:], ppair[:], tile,
                                                                          start=True, stop=True),
                         reads=["ppair", rqkv], writes=[f"pq{par}"])
                    p.op("vector", lambda v, tile=tile: v.tensor_mul(rt1[:], tile, cosb[:]),
                         reads=[rqkv, "cosb"], writes=["rt1"])
                    p.op("vector", lambda v, pp=pp: v.tensor_mul(pp[:], pp[:], sinb[:]),
                         reads=[f"pq{par}", "sinb"], writes=[f"pq{par}"])
                    p.op("vector", lambda v, tile=tile, pp=pp: v.tensor_add(tile, rt1[:], pp[:]),
                         reads=["rt1", f"pq{par}"], writes=[rqkv])
                    sp = sqb[par]
                    p.op("scalar", lambda s, tile=tile, sp=sp: s.activation(sp[:], tile, AF.Square),
                         reads=[rqkv], writes=[f"sq{par}"])
                    p.op("tensor", lambda t_, sp=sp, par=par: t_.matmul(pzb[par][:], ones128[:], sp[:],
                                                                       start=True, stop=True),
                         reads=["ones128", f"sq{par}"], writes=[f"pzb{par}"])
                    if o < 16:
                        p.op("scalar", lambda s, par=par: s.activation(sstd[par][:], pzb[par][:],
                                                                      AF.Sqrt, bias=beps_q),
                             reads=[f"pzb{par}", r_beps_q], writes=[f"sstd{par}"])
                    else:
                        p.op("scalar", lambda s, par=par: s.activation(sstd[par][:], pzb[par][:],
                                                                      AF.Sqrt, scale=1.0 / DH, bias=beps_k),
                             reads=[f"pzb{par}", r_beps_k], writes=[f"sstd{par}"])
                    p.op("vector", lambda v, par=par: v.reciprocal(rstd[par][:], sstd[par][:]),
                         reads=[f"sstd{par}"], writes=[f"rstd{par}"])
                    if o < 16:
                        p.op("vector", lambda v, tile=tile, par=par: v.tensor_mul(tile, tile, rstd[par][:]),
                             reads=[rqkv, f"rstd{par}"], writes=[rqkv])
                    else:
                        kh = o - 16
                        dst = Kring[:, 1024 * kh + 128 * s0:1024 * kh + 128 * s0 + 512]
                        p.op("vector", lambda v, tile=tile, par=par, dst=dst: v.tensor_mul(
                                dst, tile, rstd[par][:]),
                             reads=[rqkv, f"rstd{par}"], writes=["Kring"])
                else:
                    kh = o - 20
                    def tv(t_, tile=tile):
                        ins = None
                        for jj in range(4):
                            ins = t_.transpose(pt[:, P * jj:P * (jj + 1)],
                                               tile[:, P * jj:P * (jj + 1)], idb[:])
                        return ins
                    p.op("tensor", tv, reads=[rtile, "idb"], writes=["pt"])
                    def cv(v, kh=kh, s0=s0):
                        dst = bass.AP(Vring, 512 * s0 + 128 * kh, [[RING * HKV * P, P], [512, 4], [1, P]])
                        srcp = bass.AP(pt, 0, [[1024, P], [P, 4], [1, P]])
                        return v.tensor_copy(dst, srcp)
                    p.op("vector", cv, reads=["pt"], writes=["Vring"])
            for pl in (2 * n + 3, 2 * n + 4):
                if pl < 12:
                    load_plane(pl)
        for h in range(HQ):
            hp = h % 2
            kh = h // 4
            klo = max(0, 4 * q - 4)
            ks = [4 * q] + [k for k in range(klo, 4 * q + 4) if k != 4 * q]

            def lims(k):
                if k >= 4 * q:
                    return 128 * (k - 4 * q), 512
                return 0, 128 * (k - 4 * q + 5)

            def emitS(k, kp):
                w0, w1 = lims(k)
                pp = pq[kp]
                p.op("tensor", lambda t_, k=k, w0=w0, w1=w1, pp=pp, kh=kh, h=h, qkv=qkv: t_.matmul(
                        pp[:, w0:w1], Kring[:, 1024 * kh + 128 * (k % RING):1024 * kh + 128 * (k % RING) + 128],
                        qkv[:, 512 * h + w0:512 * h + w1], start=True, stop=True),
                     reads=["Kring", rqkv], writes=[f"pq{kp}"])

            emitS(ks[0], 0)
            for idx, k in enumerate(ks):
                kp = idx % 2
                w0, w1 = lims(k)
                pp = pq[kp]
                first = (idx == 0)
                last = (idx == len(ks) - 1)
                if idx + 1 < len(ks):
                    emitS(ks[idx + 1], (idx + 1) % 2)
                p.op("scalar", lambda s, w0=w0, w1=w1, kp=kp, pp=pp: s.activation(
                        ET[kp][:, w0:w1], pp[:, w0:w1], AF.Exp),
                     reads=[f"pq{kp}"], writes=[f"ET{kp}"])
                jd = k - 4 * q
                if 0 <= jd <= 3:
                    p.op("vector", lambda v, jd=jd, kp=kp: v.tensor_mul(
                            ET[kp][:, P * jd:P * (jd + 1)], ET[kp][:, P * jd:P * (jd + 1)], mdiag[:]),
                         reads=["mdiag", f"ET{kp}"], writes=[f"ET{kp}"])
                ja = k - 4 * q + 4
                if 0 <= ja <= 3:
                    p.op("vector", lambda v, ja=ja, kp=kp: v.tensor_mul(
                            ET[kp][:, P * ja:P * (ja + 1)], ET[kp][:, P * ja:P * (ja + 1)], manti[:]),
                         reads=["manti", f"ET{kp}"], writes=[f"ET{kp}"])
                p.op("tensor", lambda t_, w0=w0, w1=w1, kp=kp, hp=hp, first=first, last=last: t_.matmul(
                        pzb[hp][:, w0:w1], ones128[:], ET[kp][:, w0:w1],
                        start=first, stop=last),
                     reads=["ones128", f"ET{kp}"], writes=[f"pzb{hp}"])
                p.op("tensor", lambda t_, k=k, w0=w0, w1=w1, kp=kp, hp=hp, kh=kh, first=first, last=last: t_.matmul(
                        yAV[hp][:, w0:w1], Vring[:, 512 * (k % RING) + 128 * kh:512 * (k % RING) + 128 * (kh + 1)],
                        ET[kp][:, w0:w1], start=first, stop=last),
                     reads=["Vring", f"ET{kp}"], writes=[f"yav{hp}"])
            p.op("vector", lambda v, hp=hp: v.reciprocal(rzf[hp][:], pzb[hp][:]),
                 reads=[f"pzb{hp}"], writes=[f"rzf{hp}"])
            p.op("vector", lambda v, h=h, hp=hp: v.tensor_mul(yT[:, TQ * h:TQ * (h + 1)],
                                                              yAV[hp][:], rzf[hp][:]),
                 reads=[f"yav{hp}", f"rzf{hp}"], writes=["yT"])
        for n in range(4):
            p.op("gpsimd", lambda g_, n=n: g_.dma_start(wpb[:], d_wp[n, :, :]),
                 writes=["wpb"], dma="a1wp")
            for t in range(ntile):
                pp = pq[t % 2]
                def mpr(t_, n=n, t=t, pp=pp):
                    ins = None
                    for k in range(HQ):
                        ins = t_.matmul(pp[:], yT[:, TQ * k + P * t:TQ * k + P * (t + 1)],
                                        wpb[:, 512 * k:512 * (k + 1)], start=(k == 0), stop=(k == HQ - 1))
                    return ins
                p.op("tensor", mpr, reads=["yT", "wpb"], writes=[f"pq{t % 2}"])
                p.op("scalar", lambda s, n=n, t=t, pp=pp: s.activation(
                        ynat[:, C * t + 512 * n:C * t + 512 * (n + 1)], pp[:], AF.Copy),
                     reads=[f"pq{t % 2}"], writes=["ynat"])
        if q + 1 < nq:
            ph1(q + 1)
        for t in range(ntile):
            row = (t0 + t) * P
            ya = ynat[:, C * t:C * (t + 1)]
            emit_ln_stats(p, ya, "ynat", lt2)
            p.op("sync", lambda e, row=row: e.dma_start(xres[:], d_x[row:row + P, :]),
                 writes=["xres"], dma="a1xres")
            for hh in range(2):
                cs = slice(C // 2 * hh, C // 2 * (hh + 1))
                emit_ln_apply(p, ya[:, cs], "ynat", lt2, otile[:], "otile")
                p.op("vector", lambda v, cs=cs: v.tensor_add(otile[:], otile[:], xres[:, cs]),
                     reads=["otile", "xres"], writes=["otile"])
                p.op("sync", lambda e, row=row, cs=cs: e.dma_start(d_out[row:row + P, cs], otile[:]),
                     reads=["otile"], writes=["otile_dma"], dma="ostore")


def prep_a1_weights(w_qkv, w_proj):
    wt = np.ascontiguousarray(w_qkv.T)
    a = wt.reshape(16, 128, 24, 128)
    b = a.reshape(2, 8, 128, 6, 4, 128)
    wq = np.ascontiguousarray(b.transpose(3, 0, 2, 1, 4, 5)).reshape(12, 128, 4096)
    pt_ = np.ascontiguousarray(w_proj.T)
    wp = np.ascontiguousarray(pt_.reshape(16, 128, 4, 512).transpose(2, 1, 0, 3)).reshape(4, 128, 8192)
    return wq, wp


def rope_tables(T=TFULL, dh=128, nrep=16):
    half = dh // 2
    pos = np.arange(T, dtype=np.float64)
    inv = (10000.0 ** (-np.arange(half, dtype=np.float64) / half))
    ang = pos[:, None] * inv[None, :]
    cos = np.repeat(np.cos(ang), 2, axis=1).astype(np.float32)
    sin = np.repeat(np.sin(ang), 2, axis=1).astype(np.float32)
    return np.tile(cos, (1, nrep)), np.tile(sin, (1, nrep))


def a1_masks():
    k = np.arange(128)[:, None]
    q = np.arange(128)[None, :]
    mdiag = (q >= k).astype(np.float32)
    manti = (q < k).astype(np.float32)
    return mdiag, manti


def build_a1(T=TFULL, TC=512):
    nc = bass.Bass("TRN2", target_bir_lowering=False)
    d_x = nc.dram_tensor("x", [T, C], F32, kind="ExternalInput")
    d_wq = nc.dram_tensor("wq", [6, 128, 8192], F32, kind="ExternalInput")
    d_wp = nc.dram_tensor("wp", [4, 128, 8192], F32, kind="ExternalInput")
    d_cos = nc.dram_tensor("cosr", [T, 2048], F32, kind="ExternalInput")
    d_sin = nc.dram_tensor("sinr", [T, 2048], F32, kind="ExternalInput")
    d_md = nc.dram_tensor("mdiag", [128, 128], F32, kind="ExternalInput")
    d_ma = nc.dram_tensor("manti", [128, 128], F32, kind="ExternalInput")
    d_id = nc.dram_tensor("ident", [128, 128], F32, kind="ExternalInput")
    d_out = nc.dram_tensor("out", [T, C], F32, kind="ExternalOutput")
    with contextlib.ExitStack() as stack:
        p = Prog(nc, stack)
        emit_a1(p, d_x, d_out, d_wq, d_wp, d_cos, d_sin, d_md, d_ma, d_id, T=T, TC=TC)
        p.finalize()
        p.realize()
    return nc



NDQ_GU_BLK = 16
NI = 44
NGRP = 22
I_HID = 5632


def _dq_ops(p, v_ops, src, rsrc, k2, bits, rbits, cols, base=0, nrows=P):
    m7, s15, fb, c1p, bns = v_ops["m7"], v_ops["s15"], v_ops["fb"], v_ops["c1p"], v_ops["bns"]
    CH = 2816
    rt = [r for r in (rsrc,)]
    for c0 in range(0, cols, CH):
        cw = min(CH, cols - c0)
        sl = (slice(base, base + nrows), slice(c0, c0 + cw))
        tsl = (slice(base, base + nrows), slice(0, cw))
        p.op("vector", lambda v, sl=sl, tsl=tsl, k2=k2: v.tensor_scalar(
                m7[tsl], src[sl], 4 * k2, 7, OP.logical_shift_right, OP.bitwise_and),
             reads=rt, writes=["dq_m7"])
        p.op("vector", lambda v, sl=sl, tsl=tsl, k2=k2: v.tensor_scalar(
                s15[tsl], src[sl], 12 - 4 * k2, -32768, OP.logical_shift_left, OP.bitwise_and),
             reads=rt, writes=["dq_s15"])
        p.op("vector", lambda v, tsl=tsl: v.tensor_scalar(
                fb[tsl], m7[tsl], 64, 16128, OP.mult, OP.add),
             reads=["dq_m7"], writes=["dq_fb"])
        p.op("vector", lambda v, tsl=tsl: v.tensor_scalar(
                c1p[tsl], m7[tsl], 16064, -16128, OP.mult, OP.add),
             reads=["dq_m7"], writes=["dq_c1p"])
        p.op("vector", lambda v, tsl=tsl: v.scalar_tensor_tensor(
                bns[tsl], c1p[tsl], 0.0, fb[tsl], OP.min, OP.add),
             reads=["dq_c1p", "dq_fb"], writes=["dq_bns"])
        p.op("vector", lambda v, tsl=tsl, sl=sl: v.tensor_tensor(
                bits[sl], bns[tsl], s15[tsl], OP.bitwise_or),
             reads=["dq_bns", "dq_s15"], writes=[rbits])


def emit_dq(p, d_gusrc, d_dnsrc, d_scgu, d_scdn, d_wgu, d_wdn):
    W = 11264
    src = p.sb("dq_src", [P, W], I16)
    v_ops = {k: p.sb(f"dq_{k}", [P, 2816], I16 if k != "c1p" else F16)
             for k in ("m7", "s15", "fb", "bns", "c1p")}
    bits = p.sb("dq_bits", [P, W], I16)
    wnat = p.sb("dq_wnat", [P, W], BF16)
    wperm = p.sb("dq_wperm", [P, W], BF16)
    scgu = [p.sb(f"dq_scgu{i}", [P, W], BF16) for i in range(2)]
    scdn = [p.sb(f"dq_scdn{i}", [P, C], BF16) for i in range(6)]
    for i in range(2):
        p.op("gpsimd", lambda g, i=i: g.dma_start(scgu[i][:], d_scgu[i, :, :]),
             writes=[f"scgu{i}"], dma=f"scgu{i}")
    for i in range(6):
        p.op("gpsimd", lambda g, i=i: g.dma_start(scdn[i][:], d_scdn[i, :, :]),
             writes=[f"scdn{i}"], dma=f"scdn{i}")

    for b in range(NDQ_GU_BLK):
        k2 = (b // 2) % 4
        p.op("gpsimd", lambda g, b=b: g.dma_start(src[:], d_gusrc[b, :, :]),
             writes=["dqsrc"], dma="dqsrc")
        _dq_ops(p, v_ops, src, "dqsrc", k2, bits, "dqbits", W)
        p.op("vector", lambda v, b=b: v.tensor_mul(wnat[:], bits[:].bitcast(BF16), scgu[b % 2][:]),
             reads=["dqbits", f"scgu{b % 2}"], writes=["dqwnat"])
        def copy_perm(v):
            i1 = None
            src_ap = bass.AP(wnat, 0, [[W, P], [1024, 5], [1, 8], [8, P]])
            dst_ap = bass.AP(wperm, 0, [[W, P], [2048, 5], [256, 8], [1, P]])
            i1 = v.tensor_copy(dst_ap, src_ap)
            return i1
        p.op("vector", copy_perm, reads=["dqwnat"], writes=["dqwperm"])
        def copy_perm_mixed(v):
            src_ap = bass.AP(wnat, 5120, [[W, P], [2, 4], [1, 2], [8, 64]])
            dst_ap = bass.AP(wperm, 256 * 40, [[W, P], [256, 4], [64, 2], [1, 64]])
            return v.tensor_copy(dst_ap, src_ap)
        p.op("vector", copy_perm_mixed, reads=["dqwnat"], writes=["dqwperm"])
        def copy_perm_up(v):
            src_ap = bass.AP(wnat, 5632, [[W, P], [1024, 5], [1, 8], [8, P]])
            dst_ap = bass.AP(wperm, 128, [[W, P], [2048, 5], [256, 8], [1, P]])
            return v.tensor_copy(dst_ap, src_ap)
        p.op("vector", copy_perm_up, reads=["dqwnat"], writes=["dqwperm"])
        def copy_perm_up_mixed(v):
            src_ap = bass.AP(wnat, 5632 + 5120, [[W, P], [2, 4], [1, 2], [8, 64]])
            dst_ap = bass.AP(wperm, 256 * 40 + 128, [[W, P], [256, 4], [64, 2], [1, 64]])
            return v.tensor_copy(dst_ap, src_ap)
        p.op("vector", copy_perm_up_mixed, reads=["dqwnat"], writes=["dqwperm"])
        for g in range(NGRP):
            p.op("sync", lambda e, g=g, b=b: e.dma_start(d_wgu[g, :, b, :], wperm[:, 512 * g:512 * (g + 1)]),
                 reads=["dqwperm"], writes=["dqwgu_out"], dma="dqwgu")
    for m in range(NI):
        p.op("gpsimd", lambda g, m=m: g.dma_start(src[:, 0:C], d_dnsrc[m, :, :]),
             writes=["dqsrc"], dma="dqsrc")
        if m < 40:
            a, k = m // 8, m % 8
            _dq_ops(p, v_ops, src, "dqsrc", k % 4, bits, "dqbits", C)
            stile = a
        else:
            kp = m - 40
            _dq_ops(p, v_ops, src, "dqsrc", (2 * kp) % 4, bits, "dqbits", C, base=0, nrows=64)
            _dq_ops(p, v_ops, src, "dqsrc", (2 * kp + 1) % 4, bits, "dqbits", C, base=64, nrows=64)
            stile = 5
        p.op("vector", lambda v, stile=stile: v.tensor_mul(wnat[:, 0:C], bits[:, 0:C].bitcast(BF16),
                                                           scdn[stile][:]),
             reads=["dqbits", f"scdn{stile}"], writes=["dqwnat"])
        p.op("sync", lambda e, m=m: e.dma_start(d_wdn[:, m, :], wnat[:, 0:C]),
             reads=["dqwnat"], writes=["dqwdn_out"], dma="dqwdn")


def emit_f0(p, d_x, d_out, d_wgu, d_wdn, d_ident, T=TFULL, TC=512):
    ntile = TC // P
    nchunk = T // TC
    nC = C // P

    idb, ones = emit_consts(p, d_ident)

    xt = p.sb("f0_xt", [P, C], F32)
    lt = LNTemps(p, "f0", C)
    xn = p.sb("f0_xn", [P, C], BF16)
    xnT = p.sb("f0_xnT", [P, nC * TC], BF16)
    wbuf = [p.sb(f"f0_wb{i}", [P, 16 * 512], BF16) for i in range(2)]
    sg = [p.sb(f"f0_sg{i}", [P, TC], BF16) for i in range(2)]
    gT = p.sb("f0_gT", [P, NI * TC], BF16)
    dbuf = [p.sb(f"f0_db{i}", [P, 4 * C], BF16) for i in range(2)]
    acc = p.sb("f0_acc", [P, 16 * 512], F32)
    accbf = p.sb("f0_abf", [P, 512], BF16)
    ynat = p.sb("f0_yn", [P, ntile * C], BF16)
    otile = p.sb("f0_ot", [P, C], F32)
    s1 = p.sb("f0_s1", [P, 1], F32)
    ssq = p.sb("f0_ss", [P, 1], F32)
    sqd = p.sb("f0_sq", [P, C], BF16)
    e_my = p.sb("f0_my", [P, 1], F32)
    e_nmy = p.sb("f0_nmy", [P, 1], F32)
    e_a = p.sb("f0_a", [P, 1], F32)
    e_t = p.sb("f0_t", [P, 1], F32)
    e_v = p.sb("f0_v", [P, 1], F32)
    e_d = p.sb("f0_d", [P, 1], F32)
    e_r = p.sb("f0_r", [P, 1], F32)

    pt = p.ps("f0_pt", [P, 1024], BF16)
    pg = p.ps("f0_pg", [P, TC])
    pu = p.ps("f0_pu", [P, TC])
    pd = [p.ps(f"f0_pd{i}", [P, 512]) for i in range(2)]

    zero_c, rzero = p.constv(0.0)

    for ch in range(nchunk):
        t0 = ch * ntile
        for t in range(ntile):
            row = (t0 + t) * P
            p.op("sync", lambda e, row=row: e.dma_start(xt[:], d_x[row:row + P, :]),
                 writes=["xt"], dma="f0xt")
            emit_ln_stats(p, xt[:], "xt", lt)
            emit_ln_apply(p, xt[:], "xt", lt, xn[:], "xn")
            emit_transpose_to_fm(
                p, lambda b: xn[:, 1024 * (b % 2) + (b // 2): 1024 * (b % 2) + (b // 2) + 8 * (P - 1) + 1:8],
                nC, pt, idb, xnT, lambda b, t=t: TC * b + P * t, "xn", "xnT")
        for g in range(NGRP):
            par = g % 2
            p.op("gpsimd", lambda gg, g=g, par=par: gg.dma_start(wbuf[par][:], d_wgu[g]),
                 writes=[f"wb{par}"], dma=f"f0wb{par}")
            for pair in range(2):
                mg = 2 * g + pair
                def mmg(tt, par=par, pair=pair):
                    ins = None
                    for i in range(nC):
                        ins = tt.matmul(pg[:], wbuf[par][:, 512 * i + 256 * pair:512 * i + 256 * pair + P],
                                        xnT[:, TC * i:TC * (i + 1)], start=(i == 0), stop=(i == nC - 1))
                    return ins
                p.op("tensor", mmg, reads=[f"wb{par}", "xnT"], writes=["pg"])
                p.op("scalar", lambda s, par=par: s.activation(sg[par][:], pg[:], AF.Silu),
                     reads=["pg"], writes=[f"sg{par}"])
                def mmu(tt, par=par, pair=pair):
                    ins = None
                    for i in range(nC):
                        ins = tt.matmul(pu[:], wbuf[par][:, 512 * i + 256 * pair + P:512 * i + 256 * (pair + 1)],
                                        xnT[:, TC * i:TC * (i + 1)], start=(i == 0), stop=(i == nC - 1))
                    return ins
                p.op("tensor", mmu, reads=[f"wb{par}", "xnT"], writes=["pu"])
                p.op("vector", lambda v, mg=mg, par=par: v.tensor_mul(gT[:, TC * mg:TC * (mg + 1)],
                                                                      sg[par][:], pu[:]),
                     reads=[f"sg{par}", "pu"], writes=["gT"])
        for G in range(11):
            par = G % 2
            p.op("gpsimd", lambda gg, G=G, par=par: gg.dma_start(dbuf[par][:], d_wdn[:, 4 * G:4 * G + 4, :]),
                 writes=[f"db{par}"], dma=f"f0db{par}")
            for o in range(16):
                pp = pd[o % 2]
                def mmd(tt, G=G, o=o, par=par, pp=pp):
                    ins = None
                    for ig in range(4):
                        ins = tt.matmul(pp[:], dbuf[par][:, C * ig + P * o:C * ig + P * (o + 1)],
                                        gT[:, TC * (4 * G + ig):TC * (4 * G + ig + 1)],
                                        start=(ig == 0), stop=(ig == 3))
                    return ins
                p.op("tensor", mmd, reads=[f"db{par}", "gT"], writes=[f"pd{o % 2}"])
                if G == 0:
                    p.op("vector", lambda v, o=o, pp=pp: v.tensor_copy(acc[:, 512 * o:512 * (o + 1)], pp[:]),
                         reads=[f"pd{o % 2}"], writes=["acc"])
                else:
                    p.op("vector", lambda v, o=o, pp=pp: v.tensor_add(acc[:, 512 * o:512 * (o + 1)],
                                                                      acc[:, 512 * o:512 * (o + 1)], pp[:]),
                         reads=[f"pd{o % 2}", "acc"], writes=["acc"])
        for o in range(16):
            p.op("vector", lambda v, o=o: v.tensor_copy(accbf[:], acc[:, 512 * o:512 * (o + 1)]),
                 reads=["acc"], writes=["accbf"])
            def tfn(t_, o=o):
                ins = None
                for j in range(ntile):
                    ins = t_.transpose(pt[:, P * j:P * (j + 1)], accbf[:, P * j:P * (j + 1)], idb[:])
                return ins
            p.op("tensor", tfn, reads=["accbf", "idb"], writes=["pt"])
            def cfn(v, o=o):
                dst = bass.AP(ynat, P * o, [[ntile * C, P], [C, ntile], [1, P]])
                srcp = bass.AP(pt, 0, [[1024, P], [P, ntile], [1, P]])
                return v.tensor_copy(dst, srcp)
            p.op("vector", cfn, reads=["pt"], writes=["ynat"])
        for t in range(ntile):
            row = (t0 + t) * P
            ya = ynat[:, C * t:C * (t + 1)]
            p.op("vector", lambda v, ya=ya: v.reduce_sum(s1[:], ya, axis=AX), reads=["ynat"], writes=["fs1"])
            p.op("scalar", lambda s, ya=ya: s.activation(sqd[:], ya, AF.Square, accum_out=ssq[:]),
                 reads=["ynat"], writes=["fss", "fsq"])
            p.op("vector", lambda v: v.tensor_scalar_mul(e_my[:], s1[:], 1.0 / C), reads=["fs1"], writes=["fmy"])
            p.op("vector", lambda v: v.tensor_scalar_mul(e_nmy[:], s1[:], -1.0 / C), reads=["fs1"], writes=["fnmy"])
            p.op("vector", lambda v: v.tensor_scalar_mul(e_a[:], ssq[:], 1.0 / C), reads=["fss"], writes=["fa"])
            p.op("vector", lambda v: v.tensor_scalar(e_v[:], e_my[:], e_my[:, 0:1], None, OP.mult, OP.bypass),
                 reads=["fmy"], writes=["fv"])
            p.op("vector", lambda v: v.tensor_sub(e_v[:], e_a[:], e_v[:]), reads=["fa", "fv"], writes=["fv"])
            p.op("vector", lambda v: v.tensor_scalar(e_t[:], e_a[:], 1e-6, 1e-5, OP.add, OP.mult),
                 reads=["fa"], writes=["ft"])
            p.op("vector", lambda v: v.tensor_add(e_d[:], e_v[:], e_t[:]), reads=["fv", "ft"], writes=["fd"])
            p.op("scalar", lambda s: s.activation(e_d[:], e_d[:], AF.Sqrt), reads=["fd"], writes=["fd"])
            p.op("vector", lambda v: v.reciprocal(e_r[:], e_d[:]), reads=["fd"], writes=["fr"])
            p.op("sync", lambda e, row=row: e.dma_start(xt[:], d_x[row:row + P, :]),
                 writes=["xt"], dma="f0xt")
            p.op("vector", lambda v, ya=ya: v.tensor_scalar(otile[:], ya, e_nmy[:, 0:1], e_r[:, 0:1],
                                                            OP.add, OP.mult),
                 reads=["ynat", "fnmy", "fr"], writes=["otile"])
            p.op("vector", lambda v: v.tensor_add(otile[:], otile[:], xt[:]),
                 reads=["otile", "xt"], writes=["otile"])
            p.op("sync", lambda e, row=row: e.dma_start(d_out[row:row + P, :], otile[:]),
                 reads=["otile"], writes=["otile_dma"], dma="ostore")


def prep_f0_weights(gate_up_packed, gate_up_scales, down_packed, down_scales):
    gut = np.ascontiguousarray(gate_up_packed.T)
    g16 = gut.view(np.int16).reshape(256, 11264, 2)
    g16 = np.ascontiguousarray(g16.transpose(2, 0, 1)).reshape(512, 11264)
    gu_src = np.empty((16, 128, 11264), np.int16)
    for b in range(16):
        k = b // 2
        h = k // 4
        j0 = 128 * (b % 2)
        gu_src[b] = g16[256 * h + j0:256 * h + j0 + 128]
    dnt = np.ascontiguousarray(down_packed.T)
    d16 = dnt.view(np.int16).reshape(704, 2048, 2)
    d16 = np.ascontiguousarray(d16.transpose(2, 0, 1)).reshape(1408, 2048)
    dn_src = np.empty((44, 128, 2048), np.int16)
    for m in range(40):
        a, k = m // 8, m % 8
        h = k // 4
        dn_src[m] = d16[704 * h + 128 * a:704 * h + 128 * a + 128]
    for kp in range(4):
        m = 40 + kp
        k1, k2 = 2 * kp, 2 * kp + 1
        dn_src[m, 0:64] = d16[704 * (k1 // 4) + 640:704 * (k1 // 4) + 704]
        dn_src[m, 64:128] = d16[704 * (k2 // 4) + 640:704 * (k2 // 4) + 704]
    sg = np.repeat(gate_up_scales, 16, axis=0)
    scgu = np.stack([sg[0:128], sg[128:256]])
    sd = np.repeat(down_scales, 16, axis=0)
    scdn = np.empty((6, 128, 2048), np.float16)
    for a in range(5):
        scdn[a] = sd[128 * a:128 * (a + 1)]
    scdn[5, 0:64] = sd[640:704]
    scdn[5, 64:128] = sd[640:704]
    return gu_src, dn_src, scgu, scdn


def build_dq():
    nc = bass.Bass("TRN2", target_bir_lowering=False)
    d_gusrc = nc.dram_tensor("gusrc", [16, 128, 11264], I16, kind="ExternalInput")
    d_dnsrc = nc.dram_tensor("dnsrc", [44, 128, 2048], I16, kind="ExternalInput")
    d_scgu = nc.dram_tensor("scgu", [2, 128, 11264], F16, kind="ExternalInput")
    d_scdn = nc.dram_tensor("scdn", [6, 128, 2048], F16, kind="ExternalInput")
    d_wgu = nc.dram_tensor("wgu", [22, 128, 16, 512], BF16, kind="ExternalOutput")
    d_wdn = nc.dram_tensor("wdn", [128, 44, 2048], BF16, kind="ExternalOutput")
    with contextlib.ExitStack() as stack:
        p = Prog(nc, stack)
        emit_dq(p, d_gusrc, d_dnsrc, d_scgu, d_scdn, d_wgu, d_wdn)
        p.finalize()
        p.realize()
    return nc


def build_f0(T=TFULL, TC=512):
    nc = bass.Bass("TRN2", target_bir_lowering=False)
    d_x = nc.dram_tensor("x", [T, C], F32, kind="ExternalInput")
    d_wgu = nc.dram_tensor("wgu", [22, 128, 16, 512], BF16, kind="ExternalInput")
    d_wdn = nc.dram_tensor("wdn", [128, 44, 2048], BF16, kind="ExternalInput")
    d_id = nc.dram_tensor("ident", [128, 128], F32, kind="ExternalInput")
    d_out = nc.dram_tensor("out", [T, C], F32, kind="ExternalOutput")
    with contextlib.ExitStack() as stack:
        p = Prog(nc, stack)
        emit_f0(p, d_x, d_out, d_wgu, d_wdn, d_id, T=T, TC=TC)
        p.finalize()
        p.realize()
    return nc



def emit_cvt(p, plan):
    bufs = [p.sb(f"cvt_b{i}", [P, 8192], BF16) for i in range(4)]
    slot = 0
    for (src, dst, ntiles, cols) in plan:
        for i in range(ntiles):
            b = slot % 4
            if ntiles == 1:
                sap, dap = src[:, :], dst[:, :]
            else:
                sap, dap = src[i, :, :], dst[i, :, :]
            p.op("gpsimd", lambda e, b=b, sap=sap, cols=cols: e.dma_start(bufs[b][:, 0:cols], sap),
                 writes=[f"cvt{b}"], dma=f"cvt{b}")
            p.op("sync", lambda e, b=b, dap=dap, cols=cols: e.dma_start(dap, bufs[b][:, 0:cols]),
                 reads=[f"cvt{b}"], writes=[f"cvto{b}"], dma=f"cvto{b}")
            slot += 1



def emit_a0(p, d_x, d_out, d_wd, d_wlat, d_wp, d_bd, d_blat, d_bp, d_cosT, d_sinT,
            d_ppair, d_md, d_ident, T=TFULL, TQ=1024):
    NH = 16
    DL = 128
    nC = C // P
    ntile = TQ // P
    nq = T // TQ
    SCALE = float(1.0 / np.sqrt(128.0))

    idb, ones = emit_consts(p, d_ident)
    onescol = p.sb("a0_onec", [P, 1], BF16)
    p.op("vector", lambda v: v.memset(onescol[:], 1.0), writes=["onescol"])
    ppair = p.sb("a0_pp", [P, P], BF16)
    mdiag = p.sb("a0_md", [P, P], BF16)
    p.op("gpsimd", lambda g: g.dma_start(ppair[:], d_ppair[:, :]), writes=["ppair"], dma="ppair")
    p.op("gpsimd", lambda g: g.dma_start(mdiag[:], d_md[:, :]), writes=["mdiag"], dma="a0md")
    wlat = p.sb("a0_wlat", [P, C], BF16)
    p.op("gpsimd", lambda g: g.dma_start(wlat[:], d_wlat[:, :]), writes=["wlat"], dma="wlat")
    bd = p.sb("a0_bd", [1, C], BF16)
    blat = p.sb("a0_blat", [1, P], BF16)
    bp = p.sb("a0_bp", [1, C], BF16)
    p.op("gpsimd", lambda g: g.dma_start(bd[:], d_bd[:, :]), writes=["bd"], dma="bd")
    p.op("gpsimd", lambda g: g.dma_start(blat[:], d_blat[:, :]), writes=["blat"], dma="blat")
    p.op("gpsimd", lambda g: g.dma_start(bp[:], d_bp[:, :]), writes=["bp"], dma="bp")

    xt = [p.sb("a0_xt0", [P, C], F32)] * 2
    lt = LNTemps(p, "a0", C)
    lt2 = LNTemps(p, "a0b", C)
    xn = p.sb("a0_xn", [P, C], BF16)
    xnT = p.sb("a0_xnT", [P, nC * TQ], BF16)
    kvnat = p.sb("a0_kvn", [P, T], BF16)
    kvT = p.sb("a0_kvT", [P, TQ], BF16)
    kvrotT = p.sb("a0_kvr", [P, T], BF16)
    cosb = p.sb("a0_cos", [P, TQ], F32)
    sinb = p.sb("a0_sin", [P, TQ], F32)
    rt1 = p.sb("a0_rt1", [P, 512], F32)
    wdq = [p.sb(f"a0_wd{i}", [P, C], BF16) for i in range(2)]
    qT = p.sb("a0_qT", [P, TQ], BF16)
    qrotT = p.sb("a0_qrT", [P, NH * TQ], BF16)
    yT = p.sb("a0_yT", [P, NH * TQ], BF16)
    sq = [p.sb(f"a0_sq{i}", [P, 512], BF16) for i in range(2)]
    ET = [p.sb(f"a0_ET{i}", [P, 512], BF16) for i in range(2)]
    rzf = [p.sb(f"a0_rzf{i}", [1, 512], F32) for i in range(2)]
    rzb = [p.sb(f"a0_rzb{i}", [1, 512], BF16) for i in range(2)]
    rzbc = [p.sb(f"a0_rzc{i}", [P, 512], BF16) for i in range(2)]
    kmax2 = p.sb("a0_km", [1, 1], F32)
    qmax2 = p.sb("a0_qm", [1, 1], F32)
    n11 = p.sb("a0_n11", [1, 1], F32)
    b11 = p.sb("a0_b11", [1, 1], F32)
    negBb = p.sb("a0_nBb", [1, 1], BF16)
    negB = p.sb("a0_nB", [P, 1], F32)
    wpb = p.sb("a0_wp", [P, 16 * 512], BF16)
    otile = p.sb("a0_ot", [P, C], F32)

    pt = p.ps("a0_pt", [P, 1024], BF16)
    pq = [p.ps(f"a0_pq{i}", [P, 512]) for i in range(2)]
    pz = [p.ps(f"a0_pz{i}", [1, 512]) for i in range(2)]
    yAV = [p.ps(f"a0_yav{i}", [P, 512]) for i in range(2)]
    pn = p.ps("a0_pn", [P, 512])

    p.op("vector", lambda v: v.memset(kmax2[:], 0.0), writes=["kmax2"])

    for q in range(nq):
        t0 = q * ntile
        for t in range(ntile):
            row = (t0 + t) * P
            xp = xt[t % 2]
            p.op("sync", lambda e, row=row, xp=xp: e.dma_start(xp[:], d_x[row:row + P, :]),
                 writes=["xt0"], dma="a0xt0")
            emit_ln_stats(p, xp[:], "xt0", lt)
            emit_ln_apply(p, xp[:], "xt0", lt, xn[:], "xn")
            emit_transpose_to_fm(p, lambda b: xn[:, P * b:P * (b + 1)], nC, pt, idb,
                                 xnT, lambda b, t=t: TQ * b + P * t, "xn", "xnT")
        for t in range(ntile):
            g = t0 + t
            pp = pq[t % 2]
            def mkv(tt, t=t, pp=pp):
                ins = None
                for i in range(nC):
                    ins = tt.matmul(pp[:, 0:DL], xnT[:, TQ * i + P * t:TQ * i + P * (t + 1)],
                                    wlat[:, P * i:P * (i + 1)], start=(i == 0), stop=False)
                ins = tt.matmul(pp[:, 0:DL], ones[0:1, 0:P], blat[:], start=False, stop=True)
                return ins
            p.op("tensor", mkv, reads=["xnT", "wlat", "blat", "ones"], writes=[f"pq{t % 2}"])
            p.op("scalar", lambda s, g=g, pp=pp: s.activation(kvnat[:, P * g:P * (g + 1)],
                                                              pp[:, 0:DL], AF.Copy),
                 reads=[f"pq{t % 2}"], writes=["kvnat"])
        p.op("sync", lambda e, q=q: e.dma_start(cosb[:], d_cosT[:, TQ * q:TQ * (q + 1)]),
             writes=["cosb"], dma="a0cos")
        p.op("sync", lambda e, q=q: e.dma_start(sinb[:], d_sinT[:, TQ * q:TQ * (q + 1)]),
             writes=["sinb"], dma="a0sin")
        emit_transpose_to_fm(p, lambda b, t0=t0: kvnat[:, P * (t0 + b):P * (t0 + b + 1)], ntile,
                             pt, idb, kvT, lambda b: P * b, "kvnat", "kvT")
        for cc in range(TQ // 512):
            gsl = slice(TQ * q + 512 * cc, TQ * q + 512 * (cc + 1))
            lsl = slice(512 * cc, 512 * (cc + 1))
            pp = pq[cc % 2]
            p.op("tensor", lambda t_, lsl=lsl, pp=pp: t_.matmul(pp[:], ppair[:], kvT[:, lsl],
                                                                start=True, stop=True),
                 reads=["ppair", "kvT"], writes=[f"pq{cc % 2}"])
            p.op("vector", lambda v, lsl=lsl: v.tensor_mul(rt1[:], kvT[:, lsl], cosb[:, lsl]),
                 reads=["kvT", "cosb"], writes=["rt1"])
            p.op("vector", lambda v, lsl=lsl, pp=pp: v.tensor_mul(pp[:], pp[:], sinb[:, lsl]),
                 reads=[f"pq{cc % 2}", "sinb"], writes=[f"pq{cc % 2}"])
            p.op("vector", lambda v, gsl=gsl, pp=pp: v.tensor_add(kvrotT[:, gsl], rt1[:], pp[:]),
                 reads=["rt1", f"pq{cc % 2}"], writes=["kvrotT"])
            sp = sq[cc % 2]
            p.op("scalar", lambda s, lsl=lsl, sp=sp: s.activation(sp[:], kvT[:, lsl], AF.Square),
                 reads=["kvT"], writes=[f"sq{cc % 2}"])
            p.op("tensor", lambda t_, sp=sp: t_.matmul(pn[0:1, :], onescol[:], sp[:],
                                                       start=True, stop=True),
                 reads=["onescol", f"sq{cc % 2}"], writes=["pn"])
            p.op("vector", lambda v: v.reduce_max(n11[:], pn[0:1, :], axis=AX),
                 reads=["pn"], writes=["n11"])
            p.op("vector", lambda v: v.tensor_tensor(kmax2[:], kmax2[:], n11[:], OP.max),
                 reads=["kmax2", "n11"], writes=["kmax2"])
        p.op("vector", lambda v: v.memset(qmax2[:], 0.0), writes=["qmax2"])
        for h in range(NH):
            par = h % 2
            p.op("gpsimd", lambda g_, h=h, par=par: g_.dma_start(wdq[par][:], d_wd[h, :, :]),
                 writes=[f"wdq{par}"], dma=f"a0wd{par}")
            for cc in range(TQ // 512):
                pp = pq[cc % 2]
                def mq(tt, par=par, cc=cc, h=h, pp=pp):
                    ins = None
                    for i in range(nC):
                        ins = tt.matmul(pp[:], wdq[par][:, P * i:P * (i + 1)],
                                        xnT[:, TQ * i + 512 * cc:TQ * i + 512 * (cc + 1)],
                                        start=(i == 0), stop=False)
                    ins = tt.matmul(pp[:], bd[0:1, P * h:P * (h + 1)], ones[0:1, 0:512],
                                    start=False, stop=True)
                    return ins
                p.op("tensor", mq, reads=[f"wdq{par}", "xnT", "bd", "ones"], writes=[f"pq{cc % 2}"])
                p.op("scalar", lambda s, cc=cc, pp=pp: s.activation(qT[:, 512 * cc:512 * (cc + 1)],
                                                                    pp[:], AF.Copy, scale=SCALE),
                     reads=[f"pq{cc % 2}"], writes=["qT"])
                sp = sq[cc % 2]
                p.op("scalar", lambda s, cc=cc, sp=sp: s.activation(sp[:], qT[:, 512 * cc:512 * (cc + 1)],
                                                                    AF.Square),
                     reads=["qT"], writes=[f"sq{cc % 2}"])
            for cc in range(TQ // 512):
                lsl = slice(512 * cc, 512 * (cc + 1))
                hsl = slice(TQ * h + 512 * cc, TQ * h + 512 * (cc + 1))
                pp = pq[cc % 2]
                p.op("tensor", lambda t_, lsl=lsl, pp=pp: t_.matmul(pp[:], ppair[:], qT[:, lsl],
                                                                    start=True, stop=True),
                     reads=["ppair", "qT"], writes=[f"pq{cc % 2}"])
                p.op("vector", lambda v, lsl=lsl: v.tensor_mul(rt1[:], qT[:, lsl], cosb[:, lsl]),
                     reads=["qT", "cosb"], writes=["rt1"])
                p.op("vector", lambda v, lsl=lsl, pp=pp: v.tensor_mul(pp[:], pp[:], sinb[:, lsl]),
                     reads=[f"pq{cc % 2}", "sinb"], writes=[f"pq{cc % 2}"])
                p.op("vector", lambda v, hsl=hsl, pp=pp: v.tensor_add(qrotT[:, hsl], rt1[:], pp[:]),
                     reads=["rt1", f"pq{cc % 2}"], writes=["qrotT"])
                sp = sq[cc % 2]
                p.op("tensor", lambda t_, sp=sp: t_.matmul(pn[0:1, :], onescol[:], sp[:],
                                                           start=True, stop=True),
                     reads=["onescol", f"sq{cc % 2}"], writes=["pn"])
                p.op("vector", lambda v: v.reduce_max(n11[:], pn[0:1, :], axis=AX),
                     reads=["pn"], writes=["n11"])
                p.op("vector", lambda v: v.tensor_tensor(qmax2[:], qmax2[:], n11[:], OP.max),
                     reads=["qmax2", "n11"], writes=["qmax2"])
        p.op("vector", lambda v: v.tensor_mul(n11[:], qmax2[:], kmax2[:]),
             reads=["qmax2", "kmax2"], writes=["n11"])
        p.op("scalar", lambda s: s.activation(b11[:], n11[:], AF.Sqrt, scale=1.02),
             reads=["n11"], writes=["b11"])
        p.op("vector", lambda v: v.tensor_scalar_mul(negBb[:], b11[:], -1.0),
             reads=["b11"], writes=["negBb"])
        p.op("tensor", lambda t_: t_.matmul(pn[:, 0:1], ones[0:1, 0:P], negBb[:],
                                            start=True, stop=True),
             reads=["ones", "negBb"], writes=["pn"])
        p.op("vector", lambda v: v.tensor_copy(negB[:], pn[:, 0:1]), reads=["pn"], writes=["negB"])
        for h in range(NH):
            for qc in range(TQ // 512):
                nk = 8 * q + 4 * (qc + 1)
                dbase = 8 * q + 4 * qc
                qbase = TQ * h + 512 * qc

                def emitS(k, kp):
                    jmin = max(0, k - dbase)
                    w0 = 128 * jmin
                    pp = pq[kp]
                    p.op("tensor", lambda t_, k=k, qbase=qbase, w0=w0, pp=pp: t_.matmul(
                            pp[:, w0:512], kvrotT[:, P * k:P * (k + 1)],
                            qrotT[:, qbase + w0:qbase + 512],
                            start=True, stop=True),
                         reads=["kvrotT", "qrotT"], writes=[f"pq{kp}"])

                emitS(0, 0)
                for k in range(nk):
                    kp = k % 2
                    jmin = max(0, k - dbase)
                    w0 = 128 * jmin
                    pp = pq[kp]
                    if k + 1 < nk:
                        emitS(k + 1, (k + 1) % 2)
                    p.op("scalar", lambda s, w0=w0, kp=kp, pp=pp: s.activation(
                            ET[kp][:, w0:512], pp[:, w0:512], AF.Exp, bias=negB[:, 0:1]),
                         reads=[f"pq{kp}", "negB"], writes=[f"ET{kp}"])
                    if dbase <= k < dbase + 4:
                        j = k - dbase
                        p.op("vector", lambda v, j=j, kp=kp: v.tensor_mul(
                                ET[kp][:, P * j:P * (j + 1)], ET[kp][:, P * j:P * (j + 1)], mdiag[:]),
                             reads=["mdiag", f"ET{kp}"], writes=[f"ET{kp}"])
                    p.op("tensor", lambda t_, k=k, w0=w0, kp=kp, qc=qc, first=(k == 0), last=(k == nk - 1):
                            t_.matmul(pz[qc][0:1, w0:512], onescol[:], ET[kp][:, w0:512],
                                      start=first, stop=last),
                         reads=["onescol", f"ET{kp}"], writes=[f"pz{qc}"])
                    p.op("tensor", lambda t_, k=k, w0=w0, kp=kp, qc=qc, first=(k == 0), last=(k == nk - 1):
                            t_.matmul(yAV[qc][:, w0:512], kvnat[:, P * k:P * (k + 1)],
                                      ET[kp][:, w0:512], start=first, stop=last),
                         reads=["kvnat", f"ET{kp}"], writes=[f"yav{qc}"])
                p.op("vector", lambda v, qc=qc: v.reciprocal(rzf[qc][:], pz[qc][0:1, :]),
                     reads=[f"pz{qc}"], writes=[f"rzf{qc}"])
                p.op("vector", lambda v, qc=qc: v.tensor_copy(rzb[qc][:], rzf[qc][:]),
                     reads=[f"rzf{qc}"], writes=[f"rzb{qc}"])
            for qc in range(TQ // 512):
                qsl = slice(TQ * h + 512 * qc, TQ * h + 512 * (qc + 1))
                p.op("tensor", lambda t_, qc=qc: t_.matmul(pn[:, :], ones[0:1, 0:P], rzb[qc][:],
                                                           start=True, stop=True),
                     reads=["ones", f"rzb{qc}"], writes=["pn"])
                p.op("vector", lambda v, qc=qc: v.tensor_copy(rzbc[qc][:], pn[:, :]),
                     reads=["pn"], writes=[f"rzc{qc}"])
                p.op("vector", lambda v, qsl=qsl, qc=qc: v.tensor_mul(yT[:, qsl], yAV[qc][:], rzbc[qc][:]),
                     reads=[f"yav{qc}", f"rzc{qc}"], writes=["yT"])
        p.barrier()
        ynat = xnT
        for n in range(4):
            p.op("gpsimd", lambda g_, n=n: g_.dma_start(wpb[:], d_wp[n, :, :]),
                 writes=["wpb"], dma="a0wp")
            for t in range(ntile):
                pp = pq[t % 2]
                def mpr(t_, n=n, t=t, pp=pp):
                    ins = None
                    for k in range(NH):
                        ins = t_.matmul(pp[:], yT[:, TQ * k + P * t:TQ * k + P * (t + 1)],
                                        wpb[:, 512 * k:512 * (k + 1)], start=(k == 0), stop=False)
                    ins = t_.matmul(pp[:], ones[0:1, 0:P], bp[0:1, 512 * n:512 * (n + 1)],
                                    start=False, stop=True)
                    return ins
                p.op("tensor", mpr, reads=["yT", "wpb", "ones", "bp"], writes=[f"pq{t % 2}"])
                p.op("scalar", lambda s, n=n, t=t, pp=pp: s.activation(
                        ynat[:, C * t + 512 * n:C * t + 512 * (n + 1)], pp[:], AF.Copy),
                     reads=[f"pq{t % 2}"], writes=["ynatq"])
        for t in range(ntile):
            row = (t0 + t) * P
            ya = ynat[:, C * t:C * (t + 1)]
            xp = xt[t % 2]
            emit_ln_stats(p, ya, "ynatq", lt2)
            emit_ln_apply(p, ya, "ynatq", lt2, otile[:], "otile")
            p.op("sync", lambda e, row=row, xp=xp: e.dma_start(xp[:], d_x[row:row + P, :]),
                 writes=["xt0"], dma="a0xt0")
            p.op("vector", lambda v, xp=xp: v.tensor_add(otile[:], otile[:], xp[:]),
                 reads=["otile", "xt0"], writes=["otile"])
            p.op("sync", lambda e, row=row: e.dma_start(d_out[row:row + P, :], otile[:]),
                 reads=["otile"], writes=["otile_dma"], dma="ostore")
        p.barrier()


def prep_a0_weights(W_latent, b_latent, W_d, b_d, W_proj, b_proj):
    wd = np.ascontiguousarray(W_d.reshape(16, 128, 16, 128).transpose(2, 1, 0, 3)).reshape(16, 128, 2048)
    wlat = np.ascontiguousarray(W_latent.reshape(16, 128, 128).transpose(1, 0, 2)).reshape(128, 2048)
    wp = np.ascontiguousarray(W_proj.reshape(16, 128, 4, 512).transpose(2, 1, 0, 3)).reshape(4, 128, 8192)
    return (wd, wlat, wp, b_d.reshape(1, 2048), b_latent.reshape(1, 128), b_proj.reshape(1, 2048))


def a0_consts(T=TFULL):
    half = 64
    pos = np.arange(T, dtype=np.float64)
    inv = 10000.0 ** (-np.arange(half, dtype=np.float64) / half)
    ang = pos[:, None] * inv[None, :]
    cosT = np.repeat(np.cos(ang).T, 2, axis=0).astype(np.float32)
    sinT = np.repeat(np.sin(ang).T, 2, axis=0).astype(np.float32)
    sinT[0::2, :] *= -1.0
    pp = np.zeros((128, 128), np.float32)
    for k in range(64):
        pp[2 * k, 2 * k + 1] = 1.0
        pp[2 * k + 1, 2 * k] = 1.0
    r = np.arange(128)
    mtri = np.where(r[:, None] >= r[None, :], 0.0, -1e9).astype(np.float32)
    return cosT, sinT, pp, mtri


def build_a0(T=TFULL, TQ=1024):
    nc = bass.Bass("TRN2", target_bir_lowering=False)
    d_x = nc.dram_tensor("x", [T, C], F32, kind="ExternalInput")
    d_wd = nc.dram_tensor("wd", [16, 128, 2048], F32, kind="ExternalInput")
    d_wlat = nc.dram_tensor("wlat", [128, 2048], F32, kind="ExternalInput")
    d_wp = nc.dram_tensor("wp", [4, 128, 8192], F32, kind="ExternalInput")
    d_bd = nc.dram_tensor("bd", [1, 2048], F32, kind="ExternalInput")
    d_blat = nc.dram_tensor("blat", [1, 128], F32, kind="ExternalInput")
    d_bp = nc.dram_tensor("bp", [1, 2048], F32, kind="ExternalInput")
    d_cosT = nc.dram_tensor("cosT", [128, T], F32, kind="ExternalInput")
    d_sinT = nc.dram_tensor("sinT", [128, T], F32, kind="ExternalInput")
    d_ppair = nc.dram_tensor("ppair", [128, 128], F32, kind="ExternalInput")
    d_mtri = nc.dram_tensor("mtri", [128, 128], F32, kind="ExternalInput")
    d_id = nc.dram_tensor("ident", [128, 128], F32, kind="ExternalInput")
    d_out = nc.dram_tensor("out", [T, C], F32, kind="ExternalOutput")
    with contextlib.ExitStack() as stack:
        p = Prog(nc, stack)
        emit_a0(p, d_x, d_out, d_wd, d_wlat, d_wp, d_bd, d_blat, d_bp, d_cosT, d_sinT,
                d_ppair, d_mtri, d_id, T=T, TQ=TQ)
        p.finalize()
        p.realize()
    return nc


def build_full(T=TFULL):
    nc = bass.Bass("TRN2", target_bir_lowering=False)
    d_x = nc.dram_tensor("x", [T, C], F32, kind="ExternalInput")
    d_gusrc = nc.dram_tensor("gusrc", [16, 128, 11264], I16, kind="ExternalInput")
    d_dnsrc = nc.dram_tensor("dnsrc", [44, 128, 2048], I16, kind="ExternalInput")
    d_scgu = nc.dram_tensor("scgu", [2, 128, 11264], F16, kind="ExternalInput")
    d_scdn = nc.dram_tensor("scdn", [6, 128, 2048], F16, kind="ExternalInput")
    d_wd = nc.dram_tensor("wd", [16, 128, 2048], F32, kind="ExternalInput")
    d_wlat = nc.dram_tensor("wlat", [128, 2048], F32, kind="ExternalInput")
    d_wp0 = nc.dram_tensor("wp0", [4, 128, 8192], F32, kind="ExternalInput")
    d_bd = nc.dram_tensor("bd", [1, 2048], F32, kind="ExternalInput")
    d_blat = nc.dram_tensor("blat", [1, 128], F32, kind="ExternalInput")
    d_bp = nc.dram_tensor("bp", [1, 2048], F32, kind="ExternalInput")
    d_cosT = nc.dram_tensor("cosT", [128, T], F32, kind="ExternalInput")
    d_sinT = nc.dram_tensor("sinT", [128, T], F32, kind="ExternalInput")
    d_ppair = nc.dram_tensor("ppair", [128, 128], F32, kind="ExternalInput")
    d_mtri = nc.dram_tensor("mtri", [128, 128], F32, kind="ExternalInput")
    d_wq1 = nc.dram_tensor("wq1", [12, 128, 4096], F32, kind="ExternalInput")
    d_wp1 = nc.dram_tensor("wp1", [4, 128, 8192], F32, kind="ExternalInput")
    d_md = nc.dram_tensor("mdiag", [128, 128], F32, kind="ExternalInput")
    d_ma = nc.dram_tensor("manti", [128, 128], F32, kind="ExternalInput")
    d_wgu1 = nc.dram_tensor("wgu1", [64, 128, 4096], F32, kind="ExternalInput")
    d_wdn1 = nc.dram_tensor("wdn1", [16, 128, 8192], F32, kind="ExternalInput")
    d_id = nc.dram_tensor("ident", [128, 128], F32, kind="ExternalInput")
    d_out = nc.dram_tensor("out", [T, C], F32, kind="ExternalOutput")
    d_wgu = nc.dram_tensor("i_wgu", [22, 128, 16, 512], BF16, kind="Internal")
    d_wdnq = nc.dram_tensor("i_wdn", [128, 44, 2048], BF16, kind="Internal")
    d_x1 = nc.dram_tensor("i_x1", [T, C], F32, kind="Internal")
    d_x2 = nc.dram_tensor("i_x2", [T, C], F32, kind="Internal")
    d_x3 = nc.dram_tensor("i_x3", [T, C], F32, kind="Internal")
    d_wd_b = nc.dram_tensor("b_wd", [16, 128, 2048], BF16, kind="Internal")
    d_wp0_b = nc.dram_tensor("b_wp0", [4, 128, 8192], BF16, kind="Internal")
    d_wq1_b = nc.dram_tensor("b_wq1", [12, 128, 4096], BF16, kind="Internal")
    d_wp1_b = nc.dram_tensor("b_wp1", [4, 128, 8192], BF16, kind="Internal")
    d_wgu1_b = nc.dram_tensor("b_wgu1", [64, 128, 4096], BF16, kind="Internal")
    d_wdn1_b = nc.dram_tensor("b_wdn1", [16, 128, 8192], BF16, kind="Internal")
    cvt_plan = [
        (d_wd, d_wd_b, 16, 2048), (d_wp0, d_wp0_b, 4, 8192),
        (d_wq1, d_wq1_b, 12, 4096), (d_wp1, d_wp1_b, 4, 8192),
        (d_wgu1, d_wgu1_b, 64, 4096),
        (d_wdn1, d_wdn1_b, 16, 8192),
    ]
    with contextlib.ExitStack() as stack:
        p = Prog(nc, stack)
        with contextlib.ExitStack() as st:
            p.new_scope(st)
            emit_dq(p, d_gusrc, d_dnsrc, d_scgu, d_scdn, d_wgu, d_wdnq)
            p.barrier()
        with contextlib.ExitStack() as st:
            p.new_scope(st)
            emit_cvt(p, cvt_plan)
            p.barrier()
        with contextlib.ExitStack() as st:
            p.new_scope(st)
            emit_a0(p, d_x, d_x1, d_wd_b, d_wlat, d_wp0_b, d_bd, d_blat, d_bp, d_cosT, d_sinT,
                    d_ppair, d_md, d_id, T=T)
            p.barrier()
        with contextlib.ExitStack() as st:
            p.new_scope(st)
            emit_f0(p, d_x1, d_x2, d_wgu, d_wdnq, d_id, T=T)
            p.barrier()
        with contextlib.ExitStack() as st:
            p.new_scope(st)
            emit_a1(p, d_x2, d_x3, d_wq1_b, d_wp1_b, d_cosT, d_sinT, d_md, d_ma, d_ppair, d_id, T=T)
            p.barrier()
        with contextlib.ExitStack() as st:
            p.new_scope(st)
            emit_f1(p, d_x3, d_out, d_wgu1_b, d_wdn1_b, d_id, T=T)
            p.barrier()
        p.finalize()
        p.realize()
    return nc


_BUILT = {}


def _get(name, builder):
    if name not in _BUILT:
        _BUILT[name] = builder()
    return _BUILT[name]


def kernel(L0_attn_x, L0_attn_W_latent, L0_attn_b_latent, L0_attn_W_d, L0_attn_b_d,
           L0_attn_W_proj, L0_attn_b_proj, L0_ffn_gate_up_packed, L0_ffn_gate_up_scales,
           L0_ffn_down_packed, L0_ffn_down_scales, L1_attn_w_qkv, L1_attn_w_proj,
           L1_ffn_w_gate, L1_ffn_w_up, L1_ffn_w_down):
    cores = list(range(NCORES))
    B = L0_attn_x.shape[0]
    assert B == NCORES

    gu_src, dn_src, scgu, scdn = prep_f0_weights(
        L0_ffn_gate_up_packed, L0_ffn_gate_up_scales, L0_ffn_down_packed, L0_ffn_down_scales)
    a0w = prep_a0_weights(L0_attn_W_latent, L0_attn_b_latent, L0_attn_W_d, L0_attn_b_d,
                          L0_attn_W_proj, L0_attn_b_proj)
    wd0, wlat0, wp0, bd0, blat0, bp0 = a0w
    cosT, sinT, ppair, mtri = a0_consts(T=TFULL)
    wq1, wp1 = prep_a1_weights(L1_attn_w_qkv, L1_attn_w_proj)
    md1, ma1 = a1_masks()
    wgu1, wdn1 = prep_f1_weights(L1_ffn_w_gate, L1_ffn_w_up, L1_ffn_w_down)

    nc = _get("full", build_full)
    maps = [{"x": np.ascontiguousarray(L0_attn_x[c]),
             "gusrc": gu_src, "dnsrc": dn_src, "scgu": scgu, "scdn": scdn,
             "wd": wd0, "wlat": wlat0, "wp0": wp0, "bd": bd0, "blat": blat0, "bp": bp0,
             "cosT": cosT, "sinT": sinT, "ppair": ppair, "mtri": mtri,
             "wq1": wq1, "wp1": wp1, "mdiag": md1, "manti": ma1,
             "wgu1": wgu1, "wdn1": wdn1, "ident": IDENT} for c in cores]
    res = run_bass_kernel_spmd(nc, maps, core_ids=cores).results
    return np.stack([np.asarray(res[c]["out"], dtype=np.float32) for c in cores])


# version 11
# speedup vs baseline: 1.0114x; 1.0114x over previous
import contextlib
import numpy as np
import ml_dtypes
import concourse.bass as bass
import concourse.mybir as mybir
from concourse.bass_utils import run_bass_kernel_spmd
from concourse.alu_op_type import AluOpType as OP

F32 = mybir.dt.float32
BF16 = mybir.dt.bfloat16
F16 = mybir.dt.float16
I16 = mybir.dt.int16
I32 = mybir.dt.int32
AF = mybir.ActivationFunctionType
AX = mybir.AxisListType.X

P = 128
NCORES = 8
TFULL = 4096
C = 2048

ENGINES = ("sync", "gpsimd", "vector", "scalar", "tensor")


class Prog:

    def __init__(self, nc, stack):
        self.nc = nc
        self.stack = stack
        self.semstack = stack
        self.ops = {e: [] for e in ENGINES}
        self.sem = {e: stack.enter_context(nc.semaphore(f"sem_{e}")) for e in ENGINES}
        self.cnt = {e: 0 for e in ENGINES}
        self.satisfied = {e: {} for e in ENGINES}
        self.last_write = {}
        self.readers = {}
        self.uid = 0

    def sb(self, name, shape, dtype):
        self.uid += 1
        return self.stack.enter_context(self.nc.sbuf_tensor(f"{name}_u{self.uid}", shape, dtype))

    def ps(self, name, shape, dtype=F32):
        self.uid += 1
        return self.stack.enter_context(self.nc.psum_tensor(f"{name}_u{self.uid}", shape, dtype))

    def op(self, eng, fn, reads=(), writes=(), dma=None):
        deps = []
        for r in reads:
            lw = self.last_write.get(r)
            if lw is not None:
                deps.append(lw)
        for w in writes:
            lw = self.last_write.get(w)
            if lw is not None:
                deps.append(lw)
            deps.extend(self.readers.get(w, ()))
        bycnt = {}
        for sname, pcnt in deps:
            bycnt[sname] = max(bycnt.get(sname, 0), pcnt)
        waits = []
        for sname, pcnt in bycnt.items():
            sat = self.satisfied[eng].get(sname, 0)
            if pcnt > sat:
                self.satisfied[eng][sname] = pcnt
                waits.append((self.sem[sname], pcnt))
        if dma is not None:
            sname = "dma_" + dma
            if sname not in self.sem:
                self.sem[sname] = self.semstack.enter_context(self.nc.semaphore(sname))
                self.cnt[sname] = 0
            inc = 16
        else:
            sname = eng
            inc = 1
        self.cnt[sname] += inc
        mycnt = self.cnt[sname]
        sem_h = self.sem[sname]

        def emit(e, waits=waits, fn=fn, sem_h=sem_h, inc=inc):
            for h, v in waits:
                e.wait_ge(h, v)
            ins = fn(e)
            ins.then_inc(sem_h, inc)

        self.ops[eng].append(emit)
        tok = (sname, mycnt)
        for w in writes:
            self.last_write[w] = tok
            self.readers[w] = []
        for r in reads:
            self.readers.setdefault(r, []).append(tok)
        return tok

    def new_scope(self, stack):
        self.stack = stack
        self._consts = {}

    def constv(self, value, dtype=F32):
        key = (float(value), dtype)
        if not hasattr(self, "_consts"):
            self._consts = {}
        if key not in self._consts:
            self.uid += 1
            res = f"cpool{self.uid}"
            t = self.sb(res, [P, 1], dtype)
            self.op("vector", lambda v, t=t, value=value: v.memset(t[:], value), writes=[res])
            self._consts[key] = (t, res)
        t, res = self._consts[key]
        return t[:, 0:1], res

    def barrier(self):
        snap = dict(self.cnt)
        for eng in ENGINES:
            waits = []
            for peng, pcnt in snap.items():
                if pcnt <= 0 or peng == eng:
                    continue
                if self.satisfied[eng].get(peng, 0) < pcnt:
                    self.satisfied[eng][peng] = pcnt
                    waits.append((self.sem[peng], pcnt))
            if waits:
                def emit(e, waits=waits):
                    for h, v in waits:
                        e.wait_ge(h, v)
                self.ops[eng].append(emit)

    def finalize(self):
        waits = [(self.sem[s], self.cnt[s]) for s in self.cnt
                 if s.startswith("dma_") and self.cnt[s] > 0]
        def emit(e, waits=waits):
            for h, v in waits:
                e.wait_ge(h, v)
        self.ops["sync"].append(emit)

    def realize(self):
        with self.nc.Block() as block:
            for eng in ENGINES:
                ops = self.ops[eng]
                if not ops:
                    continue
                def runner(e, ops=ops):
                    for f in ops:
                        f(e)
                getattr(block, eng)(runner)



def emit_consts(p, d_ident, want_ones=True):
    idb = p.sb("c_idb", [P, P], BF16)
    p.op("gpsimd", lambda g: g.dma_start(idb[:], d_ident[:, :]), writes=["idb"], dma="idb")
    ones = None
    if want_ones:
        ones = p.sb("c_ones", [1, 512], BF16)
        p.op("vector", lambda v: v.memset(ones[:], 1.0), writes=["ones"])
    return idb, ones


class LNTemps:
    def __init__(self, p, tag, D, shared_sq=None):
        self.D = D
        self.s1 = p.sb(f"ln_s1_{tag}", [P, 1], F32)
        self.negmean = p.sb(f"ln_nm_{tag}", [P, 1], F32)
        self.ss = p.sb(f"ln_ss_{tag}", [P, 1], F32)
        self.std = p.sb(f"ln_sd_{tag}", [P, 1], F32)
        self.rstd = p.sb(f"ln_rs_{tag}", [P, 1], F32)
        self.sq = shared_sq if shared_sq is not None else p.sb(f"ln_sq_{tag}", [P, D], BF16)
        self.res = f"lnt_{tag}"


def emit_ln_stats(p, xin_ap, rx, lt, eps=1e-5, extra_denom=None):
    D = lt.D
    r = lt.res
    p.op("vector", lambda v: v.reduce_sum(lt.s1[:], xin_ap, axis=AX), reads=[rx], writes=[r + "s1"])
    p.op("vector", lambda v: v.tensor_scalar_mul(lt.negmean[:], lt.s1[:], -1.0 / D),
         reads=[r + "s1"], writes=[r + "nm"])
    p.op("scalar", lambda s: s.activation(lt.sq[:], xin_ap, AF.Square, bias=lt.negmean[:, 0:1],
                                          accum_out=lt.ss[:]),
         reads=[rx, r + "nm"], writes=[r + "ss", r + "sq"])
    if extra_denom is None:
        beps, reps = p.constv(eps)
        p.op("scalar", lambda s: s.activation(lt.std[:], lt.ss[:], AF.Sqrt, scale=1.0 / D, bias=beps),
             reads=[r + "ss", reps], writes=[r + "std"])
    else:
        extra_denom(p, lt)
    p.op("vector", lambda v: v.reciprocal(lt.rstd[:], lt.std[:]), reads=[r + "std"], writes=[r + "rstd"])


def emit_ln_apply(p, xin_ap, rx, lt, out_ap, rout):
    p.op("vector", lambda v: v.tensor_scalar(out_ap, xin_ap, lt.negmean[:, 0:1], lt.rstd[:, 0:1],
                                             OP.add, OP.mult),
         reads=[rx, lt.res + "nm", lt.res + "rstd"], writes=[rout])


def emit_transpose_to_fm(p, src_ap_fn, nblk, pt, idb, dst, dst_col_fn, rsrc, rdst, dtile=128):
    for g0 in range(0, nblk, 4):
        gn = min(4, nblk - g0)
        def tfn(t, g0=g0, gn=gn):
            ins = None
            for j in range(gn):
                ins = t.transpose(pt[:, dtile * j:dtile * (j + 1)], src_ap_fn(g0 + j), idb[:])
            return ins
        p.op("tensor", tfn, reads=[rsrc, "idb"], writes=["pt"])
        def cfn(v, g0=g0, gn=gn):
            ins = None
            for j in range(gn):
                ins = v.tensor_copy(dst[:, dst_col_fn(g0 + j):dst_col_fn(g0 + j) + dtile],
                                    pt[:, dtile * j:dtile * (j + 1)])
            return ins
        p.op("vector", cfn, reads=["pt"], writes=[rdst])



def emit_f1(p, d_x, d_out, d_wgu, d_wd, d_ident, T=TFULL, TC=512):
    H = 8192
    nH = H // P
    nC = C // P
    ntile = TC // P
    nchunk = T // TC

    idb, ones = emit_consts(p, d_ident)

    xt = [p.sb(f"f1_xt{i}", [P, C], F32) for i in range(ntile)]
    lt = LNTemps(p, "f1", C)
    lt2 = LNTemps(p, "f1b", C)
    xn = p.sb("f1_xn", [P, C], BF16)
    xnT = p.sb("f1_xnT", [P, nC * TC], BF16)
    wbuf = [p.sb(f"f1_wb{i}", [P, 2 * nC * P], BF16) for i in range(2)]
    sg = [p.sb(f"f1_sg{i}", [P, TC], BF16) for i in range(2)]
    g64 = p.sb("f1_g64", [P, nH * TC], BF16)
    wdbuf = [p.sb(f"f1_wd{i}", [P, nH * P], BF16) for i in range(2)]
    yT = p.sb("f1_yT", [P, nC * TC], BF16)
    ynat = p.sb("f1_yn", [P, C], BF16)
    otile = p.sb("f1_ot", [P, C], F32)

    pt = p.ps("f1_pt", [P, 512], BF16)
    pg = [p.ps(f"f1_pg{i}", [P, TC]) for i in range(2)]
    pu = [p.ps(f"f1_pu{i}", [P, TC]) for i in range(2)]
    po = [p.ps(f"f1_po{i}", [P, TC]) for i in range(2)]

    for ch in range(nchunk):
        t0 = ch * ntile
        for t in range(ntile):
            row = (t0 + t) * P
            p.op("sync", lambda e, t=t, row=row: e.dma_start(xt[t][:], d_x[row:row + P, :]),
                 writes=[f"xt{t}"], dma=f"f1xt{t}")
        for t in range(ntile):
            emit_ln_stats(p, xt[t][:], f"xt{t}", lt)
            emit_ln_apply(p, xt[t][:], f"xt{t}", lt, xn[:], "xn")
            emit_transpose_to_fm(p, lambda b: xn[:, P * b:P * (b + 1)], nC, pt, idb,
                                 xnT, lambda b, t=t: TC * b + P * t, "xn", "xnT")
        for m in range(nH):
            par = m % 2
            p.op("gpsimd", lambda g, m=m, par=par: g.dma_start(wbuf[par][:], d_wgu[m, :, :]),
                 writes=[f"wb{par}"], dma=f"f1wb{par}")
            def mmg(t, par=par):
                ins = None
                for i in range(nC):
                    ins = t.matmul(pg[par][:], wbuf[par][:, P * i:P * (i + 1)],
                                   xnT[:, TC * i:TC * (i + 1)], start=(i == 0), stop=(i == nC - 1))
                return ins
            p.op("tensor", mmg, reads=[f"wb{par}", "xnT"], writes=[f"pg{par}"])
            p.op("scalar", lambda s, m=m, par=par: s.activation(sg[par][:], pg[par][:], AF.Silu),
                 reads=[f"pg{par}"], writes=[f"sg{par}"])
            def mmu(t, par=par):
                ins = None
                for i in range(nC):
                    ins = t.matmul(pu[par][:], wbuf[par][:, C + P * i:C + P * (i + 1)],
                                   xnT[:, TC * i:TC * (i + 1)], start=(i == 0), stop=(i == nC - 1))
                return ins
            p.op("tensor", mmu, reads=[f"wb{par}", "xnT"], writes=[f"pu{par}"])
            p.op("vector", lambda v, m=m, par=par: v.tensor_mul(g64[:, TC * m:TC * (m + 1)],
                                                                sg[par][:], pu[par][:]),
                 reads=[f"sg{par}", f"pu{par}"], writes=["g64"])
        for m in range(nC):
            par = m % 2
            p.op("gpsimd", lambda g, m=m, par=par: g.dma_start(wdbuf[par][:], d_wd[m, :, :]),
                 writes=[f"wd{par}"], dma=f"f1wd{par}")
            def mmd(t, par=par):
                ins = None
                for i in range(nH):
                    ins = t.matmul(po[par][:], wdbuf[par][:, P * i:P * (i + 1)],
                                   g64[:, TC * i:TC * (i + 1)], start=(i == 0), stop=(i == nH - 1))
                return ins
            p.op("tensor", mmd, reads=[f"wd{par}", "g64"], writes=[f"po{par}"])
            p.op("scalar", lambda s, m=m, par=par: s.activation(yT[:, TC * m:TC * (m + 1)],
                                                                po[par][:], AF.Copy),
                 reads=[f"po{par}"], writes=["yT"])
        for t in range(ntile):
            row = (t0 + t) * P
            emit_transpose_to_fm(p, lambda b, t=t: yT[:, TC * b + P * t:TC * b + P * (t + 1)],
                                 nC, pt, idb, ynat, lambda b: P * b, "yT", "ynat")
            emit_ln_stats(p, ynat[:], "ynat", lt2)
            emit_ln_apply(p, ynat[:], "ynat", lt2, otile[:], "otile")
            p.op("vector", lambda v, t=t: v.tensor_add(otile[:], otile[:], xt[t][:]),
                 reads=["otile", f"xt{t}"], writes=["otile"])
            p.op("sync", lambda e, row=row: e.dma_start(d_out[row:row + P, :], otile[:]),
                 reads=["otile"], writes=["otile_dma"], dma="ostore")


def prep_f1_weights(w_gate, w_up, w_down):
    def prep_gu(w):
        wt = np.ascontiguousarray(w.T)
        a = wt.reshape(16, 128, 64, 128)
        return np.ascontiguousarray(a.transpose(2, 1, 0, 3)).reshape(64, 128, 2048)
    def prep_d(w):
        wt = np.ascontiguousarray(w.T)
        a = wt.reshape(64, 128, 16, 128)
        return np.ascontiguousarray(a.transpose(2, 1, 0, 3)).reshape(16, 128, 8192)
    wgu = np.ascontiguousarray(np.concatenate([prep_gu(w_gate), prep_gu(w_up)], axis=2))
    return wgu, prep_d(w_down)


IDENT = np.eye(128, dtype=np.float32)


def build_f1(T=TFULL, TC=512):
    nc = bass.Bass("TRN2", target_bir_lowering=False)
    d_x = nc.dram_tensor("x", [T, C], F32, kind="ExternalInput")
    d_wg = nc.dram_tensor("wg", [64, 128, 4096], F32, kind="ExternalInput")
    d_wd = nc.dram_tensor("wd", [16, 128, 8192], F32, kind="ExternalInput")
    d_id = nc.dram_tensor("ident", [128, 128], F32, kind="ExternalInput")
    d_out = nc.dram_tensor("out", [T, C], F32, kind="ExternalOutput")
    with contextlib.ExitStack() as stack:
        p = Prog(nc, stack)
        emit_f1(p, d_x, d_out, d_wg, d_wd, d_id, T=T, TC=TC)
        p.finalize()
        p.realize()
    return nc



def emit_a1(p, d_x, d_out, d_wq, d_wp, d_cosT, d_sinT, d_mdiag, d_manti, d_ppair_in,
            d_ident, T=TFULL, TQ=512):
    HQ, HKV, DH = 16, 4, 128
    nC = C // P
    ntile = TQ // P
    nq = T // TQ
    RING = 8

    idb, _ = emit_consts(p, d_ident, want_ones=False)
    ones128 = p.sb("a1_ones128", [P, P], BF16)
    p.op("vector", lambda v: v.memset(ones128[:], 1.0), writes=["ones128"])
    ppair = p.sb("a1_pp", [P, P], BF16)
    mdiag = p.sb("a1_md", [P, P], BF16)
    manti = p.sb("a1_ma", [P, P], BF16)
    p.op("gpsimd", lambda g: g.dma_start(ppair[:], d_ppair_in[:, :]), writes=["ppair"], dma="a1pp")
    p.op("gpsimd", lambda g: g.dma_start(mdiag[:], d_mdiag[:, :]), writes=["mdiag"], dma="mdiag")
    p.op("gpsimd", lambda g: g.dma_start(manti[:], d_manti[:, :]), writes=["manti"], dma="manti")

    xt = [p.sb(f"a1_xt{i}", [P, C], F32) for i in range(2)]
    xres = p.sb("a1_xr", [P, C], F32)
    lt = LNTemps(p, "a1", C)
    lt2 = LNTemps(p, "a1b", C, shared_sq=lt.sq)
    xn = p.sb("a1_xn", [P, C], BF16)
    xnT = p.sb("a1_xnT", [P, nC * TQ], BF16)
    ynat = p.sb("a1_ynat", [P, ntile * C], BF16)
    qkvT = [p.sb(f"a1_qkv{i}", [P, 20 * TQ], BF16) for i in range(2)]
    vtmp = [p.sb("a1_vt", [P, TQ], BF16)] * 2
    Kring = p.sb("a1_KR", [P, HKV * RING * P], BF16)
    Vring = p.sb("a1_VR", [P, RING * HKV * P], BF16)
    yT = p.sb("a1_yT", [P, HQ * TQ], BF16)
    wqb = [p.sb(f"a1_wq{i}", [P, 4096], BF16) for i in range(3)]
    wpb = p.sb("a1_wp", [P, 16 * 512], BF16)
    cosb = p.sb("a1_cos", [P, TQ], F32)
    sinb = p.sb("a1_sin", [P, TQ], F32)
    rt1 = p.sb("a1_rt1", [P, 512], F32)
    sqb = p.sb("a1_sq", [P, 512], BF16)
    sstd = [p.sb(f"a1_sstd{i}", [P, 512], BF16) for i in range(2)]
    rstd = [p.sb(f"a1_rstd{i}", [P, 512], F32) for i in range(2)]
    ET = [p.sb(f"a1_ET{i}", [P, 512], BF16) for i in range(4)]
    rzf = [p.sb(f"a1_rzf{i}", [P, 512], F32) for i in range(2)]
    otile = p.sb("a1_ot", [P, C // 2], F32)

    pt = p.ps("a1_pt", [P, 1024], BF16)
    pq = [p.ps(f"a1_pq{i}", [P, 512]) for i in range(3)]
    pzb = [p.ps(f"a1_pz{i}", [P, 512]) for i in range(2)]
    yAV = [p.ps(f"a1_yav{i}", [P, 512]) for i in range(2)]

    beps_q, r_beps_q = p.constv(128.0 * 1.1920929e-07)
    beps_k, r_beps_k = p.constv(1.1920929e-07)

    def ph1(qq):
        for t in range(ntile):
            row = (qq * ntile + t) * P
            xp = xt[t % 2]
            p.op("sync", lambda e, row=row, xp=xp: e.dma_start(xp[:], d_x[row:row + P, :]),
                 writes=["xt0"], dma="a1xt0")
            emit_ln_stats(p, xp[:], "xt0", lt)
            emit_ln_apply(p, xp[:], "xt0", lt, xn[:], "xn")
            emit_transpose_to_fm(p, lambda b: xn[:, P * b:P * (b + 1)], nC, pt, idb,
                                 xnT, lambda b, t=t: TQ * b + P * t, "xn", "xnT")

    def load_plane(pl):
        wb = wqb[pl % 3]
        p.op("gpsimd", lambda g_, pl=pl, wb=wb: g_.dma_start(wb[:], d_wq[pl, :, :]),
             writes=[f"wq{pl % 3}"], dma=f"a1wq{pl % 3}")

    ph1(0)
    for pl in range(3):
        load_plane(pl)
    for q in range(nq):
        t0 = q * ntile
        qpar = q % 2
        qkv = qkvT[qpar]
        rqkv = f"qkv{qpar}"
        s0 = (4 * q) % RING
        p.op("sync", lambda e, q=q: e.dma_start(cosb[:], d_cosT[:, TQ * q:TQ * (q + 1)]),
             writes=["cosb"], dma="a1cos")
        p.op("sync", lambda e, q=q: e.dma_start(sinb[:], d_sinT[:, TQ * q:TQ * (q + 1)]),
             writes=["sinb"], dma="a1sin")

        for n in range(6):
            for j in range(4):
                o = 4 * n + j
                par = o % 2
                p3 = o % 3
                pp = pq[p3]
                def mq(tt, n=n, j=j, pp=pp):
                    ins = None
                    for i in range(nC):
                        wb = wqb[(2 * n + i // 8) % 3]
                        ins = tt.matmul(pp[:], wb[:, 512 * (i % 8) + 128 * j:512 * (i % 8) + 128 * (j + 1)],
                                        xnT[:, TQ * i:TQ * (i + 1)], start=(i == 0), stop=(i == nC - 1))
                    return ins
                p.op("tensor", mq, reads=[f"wq{(2 * n) % 3}", f"wq{(2 * n + 1) % 3}", "xnT"],
                     writes=[f"pq{p3}"])
                if o < 20:
                    tile = qkv[:, 512 * o:512 * (o + 1)]
                    rtile = rqkv
                else:
                    tile = vtmp[par][:]
                    rtile = "vt"
                p.op("scalar", lambda s, tile=tile, pp=pp: s.activation(tile, pp[:], AF.Copy),
                     reads=[f"pq{p3}"], writes=[rtile])
                if o < 20:
                    p.op("tensor", lambda t_, tile=tile, pp=pp: t_.matmul(pp[:], ppair[:], tile,
                                                                          start=True, stop=True),
                         reads=["ppair", rqkv], writes=[f"pq{p3}"])
                    p.op("vector", lambda v, tile=tile: v.tensor_mul(rt1[:], tile, cosb[:]),
                         reads=[rqkv, "cosb"], writes=["rt1"])
                    p.op("vector", lambda v, pp=pp: v.tensor_mul(pp[:], pp[:], sinb[:]),
                         reads=[f"pq{p3}", "sinb"], writes=[f"pq{p3}"])
                    p.op("vector", lambda v, tile=tile, pp=pp: v.tensor_add(tile, rt1[:], pp[:]),
                         reads=["rt1", f"pq{p3}"], writes=[rqkv])
                    p.op("scalar", lambda s, tile=tile: s.activation(sqb[:], tile, AF.Square),
                         reads=[rqkv], writes=["sq"])
                    p.op("tensor", lambda t_, par=par: t_.matmul(pzb[par][:], ones128[:], sqb[:],
                                                                start=True, stop=True),
                         reads=["ones128", "sq"], writes=[f"pzb{par}"])
                    if o < 16:
                        p.op("scalar", lambda s, par=par: s.activation(sstd[par][:], pzb[par][:],
                                                                      AF.Sqrt, bias=beps_q),
                             reads=[f"pzb{par}", r_beps_q], writes=[f"sstd{par}"])
                    else:
                        p.op("scalar", lambda s, par=par: s.activation(sstd[par][:], pzb[par][:],
                                                                      AF.Sqrt, scale=1.0 / DH, bias=beps_k),
                             reads=[f"pzb{par}", r_beps_k], writes=[f"sstd{par}"])
                    p.op("vector", lambda v, par=par: v.reciprocal(rstd[par][:], sstd[par][:]),
                         reads=[f"sstd{par}"], writes=[f"rstd{par}"])
                    if o < 16:
                        p.op("vector", lambda v, tile=tile, par=par: v.tensor_mul(tile, tile, rstd[par][:]),
                             reads=[rqkv, f"rstd{par}"], writes=[rqkv])
                    else:
                        kh = o - 16
                        dst = Kring[:, 1024 * kh + 128 * s0:1024 * kh + 128 * s0 + 512]
                        p.op("vector", lambda v, tile=tile, par=par, dst=dst: v.tensor_mul(
                                dst, tile, rstd[par][:]),
                             reads=[rqkv, f"rstd{par}"], writes=["Kring"])
                else:
                    kh = o - 20
                    def tv(t_, tile=tile):
                        ins = None
                        for jj in range(4):
                            ins = t_.transpose(pt[:, P * jj:P * (jj + 1)],
                                               tile[:, P * jj:P * (jj + 1)], idb[:])
                        return ins
                    p.op("tensor", tv, reads=[rtile, "idb"], writes=["pt"])
                    def cv(v, kh=kh, s0=s0):
                        dst = bass.AP(Vring, 512 * s0 + 128 * kh, [[RING * HKV * P, P], [512, 4], [1, P]])
                        srcp = bass.AP(pt, 0, [[1024, P], [P, 4], [1, P]])
                        return v.tensor_copy(dst, srcp)
                    p.op("vector", cv, reads=["pt"], writes=["Vring"])
            for pl in (2 * n + 3, 2 * n + 4):
                if pl < 12:
                    load_plane(pl)
        def lims(k):
            if k >= 4 * q:
                return 128 * (k - 4 * q), 512
            return 0, 128 * (k - 4 * q + 5)

        klo = max(0, 4 * q - 4)
        ks = [4 * q] + [k for k in range(klo, 4 * q + 4) if k != 4 * q]
        nk = len(ks)
        for hpair in range(0, HQ, 2):
            sct = [0]
            pqmap = {}

            def emitS(h, idx):
                k = ks[idx]
                w0, w1 = lims(k)
                b = sct[0] % 3
                sct[0] += 1
                pqmap[(h, idx)] = b
                pp = pq[b]
                kh = h // 4
                p.op("tensor", lambda t_, k=k, w0=w0, w1=w1, pp=pp, kh=kh, h=h, qkv=qkv: t_.matmul(
                        pp[:, w0:w1], Kring[:, 1024 * kh + 128 * (k % RING):1024 * kh + 128 * (k % RING) + 128],
                        qkv[:, 512 * h + w0:512 * h + w1], start=True, stop=True),
                     reads=["Kring", rqkv], writes=[f"pq{b}"])

            emitS(hpair, 0)
            emitS(hpair + 1, 0)
            for idx in range(nk):
                k = ks[idx]
                w0, w1 = lims(k)
                first = (idx == 0)
                last = (idx == nk - 1)
                for dh in (0, 1):
                    h = hpair + dh
                    hp = h % 2
                    kh = h // 4
                    if idx + 1 < nk:
                        emitS(h, idx + 1)
                    b = pqmap[(h, idx)]
                    pp = pq[b]
                    eti = 2 * dh + idx % 2
                    et = ET[eti]
                    ret = f"ET{eti}"
                    p.op("scalar", lambda s, w0=w0, w1=w1, pp=pp, et=et: s.activation(
                            et[:, w0:w1], pp[:, w0:w1], AF.Exp),
                         reads=[f"pq{b}"], writes=[ret])
                    jd = k - 4 * q
                    if 0 <= jd <= 3:
                        p.op("vector", lambda v, jd=jd, et=et: v.tensor_mul(
                                et[:, P * jd:P * (jd + 1)], et[:, P * jd:P * (jd + 1)], mdiag[:]),
                             reads=["mdiag", ret], writes=[ret])
                    ja = k - 4 * q + 4
                    if 0 <= ja <= 3:
                        p.op("vector", lambda v, ja=ja, et=et: v.tensor_mul(
                                et[:, P * ja:P * (ja + 1)], et[:, P * ja:P * (ja + 1)], manti[:]),
                             reads=["manti", ret], writes=[ret])
                    p.op("tensor", lambda t_, w0=w0, w1=w1, et=et, hp=hp, first=first, last=last: t_.matmul(
                            pzb[hp][:, w0:w1], ones128[:], et[:, w0:w1],
                            start=first, stop=last),
                         reads=["ones128", ret], writes=[f"pzb{hp}"])
                    p.op("tensor", lambda t_, k=k, w0=w0, w1=w1, et=et, hp=hp, kh=kh, first=first, last=last: t_.matmul(
                            yAV[hp][:, w0:w1], Vring[:, 512 * (k % RING) + 128 * kh:512 * (k % RING) + 128 * (kh + 1)],
                            et[:, w0:w1], start=first, stop=last),
                         reads=["Vring", ret], writes=[f"yav{hp}"])
            for dh in (0, 1):
                h = hpair + dh
                hp = h % 2
                p.op("vector", lambda v, hp=hp: v.reciprocal(rzf[hp][:], pzb[hp][:]),
                     reads=[f"pzb{hp}"], writes=[f"rzf{hp}"])
                p.op("vector", lambda v, h=h, hp=hp: v.tensor_mul(yT[:, TQ * h:TQ * (h + 1)],
                                                                  yAV[hp][:], rzf[hp][:]),
                     reads=[f"yav{hp}", f"rzf{hp}"], writes=["yT"])
        for n in range(4):
            p.op("gpsimd", lambda g_, n=n: g_.dma_start(wpb[:], d_wp[n, :, :]),
                 writes=["wpb"], dma="a1wp")
            for t in range(ntile):
                b = (4 * n + t) % 3
                pp = pq[b]
                def mpr(t_, n=n, t=t, pp=pp):
                    ins = None
                    for k in range(HQ):
                        ins = t_.matmul(pp[:], yT[:, TQ * k + P * t:TQ * k + P * (t + 1)],
                                        wpb[:, 512 * k:512 * (k + 1)], start=(k == 0), stop=(k == HQ - 1))
                    return ins
                p.op("tensor", mpr, reads=["yT", "wpb"], writes=[f"pq{b}"])
                p.op("scalar", lambda s, n=n, t=t, pp=pp: s.activation(
                        ynat[:, C * t + 512 * n:C * t + 512 * (n + 1)], pp[:], AF.Copy),
                     reads=[f"pq{b}"], writes=["ynat"])
        if q + 1 < nq:
            for pl in range(3):
                load_plane(pl)
        if q + 1 < nq:
            ph1(q + 1)
        for t in range(ntile):
            row = (t0 + t) * P
            ya = ynat[:, C * t:C * (t + 1)]
            emit_ln_stats(p, ya, "ynat", lt2)
            p.op("sync", lambda e, row=row: e.dma_start(xres[:], d_x[row:row + P, :]),
                 writes=["xres"], dma="a1xres")
            for hh in range(2):
                cs = slice(C // 2 * hh, C // 2 * (hh + 1))
                p.op("gpsimd", lambda g, ya=ya, cs=cs: g.tensor_scalar(
                        otile[:], ya[:, cs], lt2.negmean[:, 0:1], lt2.rstd[:, 0:1],
                        OP.add, OP.mult),
                     reads=["ynat", lt2.res + "nm", lt2.res + "rstd"], writes=["otile"])
                p.op("gpsimd", lambda g, cs=cs: g.tensor_add(otile[:], otile[:], xres[:, cs]),
                     reads=["otile", "xres"], writes=["otile"])
                p.op("sync", lambda e, row=row, cs=cs: e.dma_start(d_out[row:row + P, cs], otile[:]),
                     reads=["otile"], writes=["otile_dma"], dma="ostore")


def prep_a1_weights(w_qkv, w_proj):
    wt = np.ascontiguousarray(w_qkv.T)
    a = wt.reshape(16, 128, 24, 128)
    b = a.reshape(2, 8, 128, 6, 4, 128)
    wq = np.ascontiguousarray(b.transpose(3, 0, 2, 1, 4, 5)).reshape(12, 128, 4096)
    pt_ = np.ascontiguousarray(w_proj.T)
    wp = np.ascontiguousarray(pt_.reshape(16, 128, 4, 512).transpose(2, 1, 0, 3)).reshape(4, 128, 8192)
    return wq, wp


def rope_tables(T=TFULL, dh=128, nrep=16):
    half = dh // 2
    pos = np.arange(T, dtype=np.float64)
    inv = (10000.0 ** (-np.arange(half, dtype=np.float64) / half))
    ang = pos[:, None] * inv[None, :]
    cos = np.repeat(np.cos(ang), 2, axis=1).astype(np.float32)
    sin = np.repeat(np.sin(ang), 2, axis=1).astype(np.float32)
    return np.tile(cos, (1, nrep)), np.tile(sin, (1, nrep))


def a1_masks():
    k = np.arange(128)[:, None]
    q = np.arange(128)[None, :]
    mdiag = (q >= k).astype(np.float32)
    manti = (q < k).astype(np.float32)
    return mdiag, manti


def build_a1(T=TFULL, TC=512):
    nc = bass.Bass("TRN2", target_bir_lowering=False)
    d_x = nc.dram_tensor("x", [T, C], F32, kind="ExternalInput")
    d_wq = nc.dram_tensor("wq", [6, 128, 8192], F32, kind="ExternalInput")
    d_wp = nc.dram_tensor("wp", [4, 128, 8192], F32, kind="ExternalInput")
    d_cos = nc.dram_tensor("cosr", [T, 2048], F32, kind="ExternalInput")
    d_sin = nc.dram_tensor("sinr", [T, 2048], F32, kind="ExternalInput")
    d_md = nc.dram_tensor("mdiag", [128, 128], F32, kind="ExternalInput")
    d_ma = nc.dram_tensor("manti", [128, 128], F32, kind="ExternalInput")
    d_id = nc.dram_tensor("ident", [128, 128], F32, kind="ExternalInput")
    d_out = nc.dram_tensor("out", [T, C], F32, kind="ExternalOutput")
    with contextlib.ExitStack() as stack:
        p = Prog(nc, stack)
        emit_a1(p, d_x, d_out, d_wq, d_wp, d_cos, d_sin, d_md, d_ma, d_id, T=T, TC=TC)
        p.finalize()
        p.realize()
    return nc



NDQ_GU_BLK = 16
NI = 44
NGRP = 22
I_HID = 5632


def _dq_ops(p, v_ops, src, rsrc, k2, bits, rbits, cols, base=0, nrows=P):
    m7, s15, fb, c1p, bns = v_ops["m7"], v_ops["s15"], v_ops["fb"], v_ops["c1p"], v_ops["bns"]
    CH = 2816
    rt = [r for r in (rsrc,)]
    for c0 in range(0, cols, CH):
        cw = min(CH, cols - c0)
        sl = (slice(base, base + nrows), slice(c0, c0 + cw))
        tsl = (slice(base, base + nrows), slice(0, cw))
        p.op("vector", lambda v, sl=sl, tsl=tsl, k2=k2: v.tensor_scalar(
                m7[tsl], src[sl], 4 * k2, 7, OP.logical_shift_right, OP.bitwise_and),
             reads=rt, writes=["dq_m7"])
        p.op("vector", lambda v, sl=sl, tsl=tsl, k2=k2: v.tensor_scalar(
                s15[tsl], src[sl], 12 - 4 * k2, -32768, OP.logical_shift_left, OP.bitwise_and),
             reads=rt, writes=["dq_s15"])
        p.op("vector", lambda v, tsl=tsl: v.tensor_scalar(
                fb[tsl], m7[tsl], 64, 16128, OP.mult, OP.add),
             reads=["dq_m7"], writes=["dq_fb"])
        p.op("vector", lambda v, tsl=tsl: v.tensor_scalar(
                c1p[tsl], m7[tsl], 16064, -16128, OP.mult, OP.add),
             reads=["dq_m7"], writes=["dq_c1p"])
        p.op("vector", lambda v, tsl=tsl: v.scalar_tensor_tensor(
                bns[tsl], c1p[tsl], 0.0, fb[tsl], OP.min, OP.add),
             reads=["dq_c1p", "dq_fb"], writes=["dq_bns"])
        p.op("vector", lambda v, tsl=tsl, sl=sl: v.tensor_tensor(
                bits[sl], bns[tsl], s15[tsl], OP.bitwise_or),
             reads=["dq_bns", "dq_s15"], writes=[rbits])


def emit_dq(p, d_gusrc, d_dnsrc, d_scgu, d_scdn, d_wgu, d_wdn, pump=None):
    W = 11264
    src = p.sb("dq_src", [P, W], I16)
    v_ops = {k: p.sb(f"dq_{k}", [P, 2816], I16 if k != "c1p" else F16)
             for k in ("m7", "s15", "fb", "bns", "c1p")}
    bits = p.sb("dq_bits", [P, W], I16)
    wnat = p.sb("dq_wnat", [P, W], BF16)
    wperm = p.sb("dq_wperm", [P, W], BF16)
    scgu = [p.sb(f"dq_scgu{i}", [P, W], BF16) for i in range(2)]
    scdn = [p.sb(f"dq_scdn{i}", [P, C], BF16) for i in range(6)]
    for i in range(2):
        p.op("gpsimd", lambda g, i=i: g.dma_start(scgu[i][:], d_scgu[i, :, :]),
             writes=[f"scgu{i}"], dma=f"scgu{i}")
    for i in range(6):
        p.op("gpsimd", lambda g, i=i: g.dma_start(scdn[i][:], d_scdn[i, :, :]),
             writes=[f"scdn{i}"], dma=f"scdn{i}")

    for b in range(NDQ_GU_BLK):
        k2 = (b // 2) % 4
        p.op("gpsimd", lambda g, b=b: g.dma_start(src[:], d_gusrc[b, :, :]),
             writes=["dqsrc"], dma="dqsrc")
        _dq_ops(p, v_ops, src, "dqsrc", k2, bits, "dqbits", W)
        p.op("vector", lambda v, b=b: v.tensor_mul(wnat[:], bits[:].bitcast(BF16), scgu[b % 2][:]),
             reads=["dqbits", f"scgu{b % 2}"], writes=["dqwnat"])
        def copy_perm(v):
            i1 = None
            src_ap = bass.AP(wnat, 0, [[W, P], [1024, 5], [1, 8], [8, P]])
            dst_ap = bass.AP(wperm, 0, [[W, P], [2048, 5], [256, 8], [1, P]])
            i1 = v.tensor_copy(dst_ap, src_ap)
            return i1
        p.op("vector", copy_perm, reads=["dqwnat"], writes=["dqwperm"])
        def copy_perm_mixed(v):
            src_ap = bass.AP(wnat, 5120, [[W, P], [2, 4], [1, 2], [8, 64]])
            dst_ap = bass.AP(wperm, 256 * 40, [[W, P], [256, 4], [64, 2], [1, 64]])
            return v.tensor_copy(dst_ap, src_ap)
        p.op("vector", copy_perm_mixed, reads=["dqwnat"], writes=["dqwperm"])
        def copy_perm_up(v):
            src_ap = bass.AP(wnat, 5632, [[W, P], [1024, 5], [1, 8], [8, P]])
            dst_ap = bass.AP(wperm, 128, [[W, P], [2048, 5], [256, 8], [1, P]])
            return v.tensor_copy(dst_ap, src_ap)
        p.op("vector", copy_perm_up, reads=["dqwnat"], writes=["dqwperm"])
        def copy_perm_up_mixed(v):
            src_ap = bass.AP(wnat, 5632 + 5120, [[W, P], [2, 4], [1, 2], [8, 64]])
            dst_ap = bass.AP(wperm, 256 * 40 + 128, [[W, P], [256, 4], [64, 2], [1, 64]])
            return v.tensor_copy(dst_ap, src_ap)
        p.op("vector", copy_perm_up_mixed, reads=["dqwnat"], writes=["dqwperm"])
        for g in range(NGRP):
            p.op("sync", lambda e, g=g, b=b: e.dma_start(d_wgu[g, :, b, :], wperm[:, 512 * g:512 * (g + 1)]),
                 reads=["dqwperm"], writes=["dqwgu_out"], dma="dqwgu")
        if pump:
            pump()
    for m in range(NI):
        p.op("gpsimd", lambda g, m=m: g.dma_start(src[:, 0:C], d_dnsrc[m, :, :]),
             writes=["dqsrc"], dma="dqsrc")
        if m < 40:
            a, k = m // 8, m % 8
            _dq_ops(p, v_ops, src, "dqsrc", k % 4, bits, "dqbits", C)
            stile = a
        else:
            kp = m - 40
            _dq_ops(p, v_ops, src, "dqsrc", (2 * kp) % 4, bits, "dqbits", C, base=0, nrows=64)
            _dq_ops(p, v_ops, src, "dqsrc", (2 * kp + 1) % 4, bits, "dqbits", C, base=64, nrows=64)
            stile = 5
        p.op("vector", lambda v, stile=stile: v.tensor_mul(wnat[:, 0:C], bits[:, 0:C].bitcast(BF16),
                                                           scdn[stile][:]),
             reads=["dqbits", f"scdn{stile}"], writes=["dqwnat"])
        p.op("sync", lambda e, m=m: e.dma_start(d_wdn[:, m, :], wnat[:, 0:C]),
             reads=["dqwnat"], writes=["dqwdn_out"], dma="dqwdn")
        if pump:
            pump()


def emit_f0(p, d_x, d_out, d_wgu, d_wdn, d_ident, T=TFULL, TC=512):
    ntile = TC // P
    nchunk = T // TC
    nC = C // P

    idb, ones = emit_consts(p, d_ident)

    xt = p.sb("f0_xt", [P, C], F32)
    lt = LNTemps(p, "f0", C)
    xn = p.sb("f0_xn", [P, C], BF16)
    xnT = p.sb("f0_xnT", [P, nC * TC], BF16)
    wbuf = [p.sb(f"f0_wb{i}", [P, 16 * 512], BF16) for i in range(2)]
    sg = [p.sb(f"f0_sg{i}", [P, TC], BF16) for i in range(2)]
    gT = p.sb("f0_gT", [P, NI * TC], BF16)
    dbuf = [p.sb(f"f0_db{i}", [P, 4 * C], BF16) for i in range(2)]
    acc = p.sb("f0_acc", [P, 16 * 512], F32)
    accbf = p.sb("f0_abf", [P, 512], BF16)
    ynat = p.sb("f0_yn", [P, ntile * C], BF16)
    otile = p.sb("f0_ot", [P, C], F32)
    s1 = p.sb("f0_s1", [P, 1], F32)
    ssq = p.sb("f0_ss", [P, 1], F32)
    sqd = p.sb("f0_sq", [P, C], BF16)
    e_my = p.sb("f0_my", [P, 1], F32)
    e_nmy = p.sb("f0_nmy", [P, 1], F32)
    e_a = p.sb("f0_a", [P, 1], F32)
    e_t = p.sb("f0_t", [P, 1], F32)
    e_v = p.sb("f0_v", [P, 1], F32)
    e_d = p.sb("f0_d", [P, 1], F32)
    e_r = p.sb("f0_r", [P, 1], F32)

    pt = p.ps("f0_pt", [P, 1024], BF16)
    pg = p.ps("f0_pg", [P, TC])
    pu = p.ps("f0_pu", [P, TC])
    pd = [p.ps(f"f0_pd{i}", [P, 512]) for i in range(2)]

    zero_c, rzero = p.constv(0.0)

    for ch in range(nchunk):
        t0 = ch * ntile
        for t in range(ntile):
            row = (t0 + t) * P
            p.op("sync", lambda e, row=row: e.dma_start(xt[:], d_x[row:row + P, :]),
                 writes=["xt"], dma="f0xt")
            emit_ln_stats(p, xt[:], "xt", lt)
            emit_ln_apply(p, xt[:], "xt", lt, xn[:], "xn")
            emit_transpose_to_fm(
                p, lambda b: xn[:, 1024 * (b % 2) + (b // 2): 1024 * (b % 2) + (b // 2) + 8 * (P - 1) + 1:8],
                nC, pt, idb, xnT, lambda b, t=t: TC * b + P * t, "xn", "xnT")
        for g in range(NGRP):
            par = g % 2
            p.op("gpsimd", lambda gg, g=g, par=par: gg.dma_start(wbuf[par][:], d_wgu[g]),
                 writes=[f"wb{par}"], dma=f"f0wb{par}")
            for pair in range(2):
                mg = 2 * g + pair
                def mmg(tt, par=par, pair=pair):
                    ins = None
                    for i in range(nC):
                        ins = tt.matmul(pg[:], wbuf[par][:, 512 * i + 256 * pair:512 * i + 256 * pair + P],
                                        xnT[:, TC * i:TC * (i + 1)], start=(i == 0), stop=(i == nC - 1))
                    return ins
                p.op("tensor", mmg, reads=[f"wb{par}", "xnT"], writes=["pg"])
                p.op("scalar", lambda s, par=par: s.activation(sg[par][:], pg[:], AF.Silu),
                     reads=["pg"], writes=[f"sg{par}"])
                def mmu(tt, par=par, pair=pair):
                    ins = None
                    for i in range(nC):
                        ins = tt.matmul(pu[:], wbuf[par][:, 512 * i + 256 * pair + P:512 * i + 256 * (pair + 1)],
                                        xnT[:, TC * i:TC * (i + 1)], start=(i == 0), stop=(i == nC - 1))
                    return ins
                p.op("tensor", mmu, reads=[f"wb{par}", "xnT"], writes=["pu"])
                p.op("vector", lambda v, mg=mg, par=par: v.tensor_mul(gT[:, TC * mg:TC * (mg + 1)],
                                                                      sg[par][:], pu[:]),
                     reads=[f"sg{par}", "pu"], writes=["gT"])
        for G in range(11):
            par = G % 2
            p.op("gpsimd", lambda gg, G=G, par=par: gg.dma_start(dbuf[par][:], d_wdn[:, 4 * G:4 * G + 4, :]),
                 writes=[f"db{par}"], dma=f"f0db{par}")
            for o in range(16):
                pp = pd[o % 2]
                def mmd(tt, G=G, o=o, par=par, pp=pp):
                    ins = None
                    for ig in range(4):
                        ins = tt.matmul(pp[:], dbuf[par][:, C * ig + P * o:C * ig + P * (o + 1)],
                                        gT[:, TC * (4 * G + ig):TC * (4 * G + ig + 1)],
                                        start=(ig == 0), stop=(ig == 3))
                    return ins
                p.op("tensor", mmd, reads=[f"db{par}", "gT"], writes=[f"pd{o % 2}"])
                if G == 0:
                    p.op("vector", lambda v, o=o, pp=pp: v.tensor_copy(acc[:, 512 * o:512 * (o + 1)], pp[:]),
                         reads=[f"pd{o % 2}"], writes=["acc"])
                else:
                    p.op("vector", lambda v, o=o, pp=pp: v.tensor_add(acc[:, 512 * o:512 * (o + 1)],
                                                                      acc[:, 512 * o:512 * (o + 1)], pp[:]),
                         reads=[f"pd{o % 2}", "acc"], writes=["acc"])
        for o in range(16):
            p.op("vector", lambda v, o=o: v.tensor_copy(accbf[:], acc[:, 512 * o:512 * (o + 1)]),
                 reads=["acc"], writes=["accbf"])
            def tfn(t_, o=o):
                ins = None
                for j in range(ntile):
                    ins = t_.transpose(pt[:, P * j:P * (j + 1)], accbf[:, P * j:P * (j + 1)], idb[:])
                return ins
            p.op("tensor", tfn, reads=["accbf", "idb"], writes=["pt"])
            def cfn(v, o=o):
                dst = bass.AP(ynat, P * o, [[ntile * C, P], [C, ntile], [1, P]])
                srcp = bass.AP(pt, 0, [[1024, P], [P, ntile], [1, P]])
                return v.tensor_copy(dst, srcp)
            p.op("vector", cfn, reads=["pt"], writes=["ynat"])
        for t in range(ntile):
            row = (t0 + t) * P
            ya = ynat[:, C * t:C * (t + 1)]
            p.op("vector", lambda v, ya=ya: v.reduce_sum(s1[:], ya, axis=AX), reads=["ynat"], writes=["fs1"])
            p.op("scalar", lambda s, ya=ya: s.activation(sqd[:], ya, AF.Square, accum_out=ssq[:]),
                 reads=["ynat"], writes=["fss", "fsq"])
            p.op("vector", lambda v: v.tensor_scalar_mul(e_my[:], s1[:], 1.0 / C), reads=["fs1"], writes=["fmy"])
            p.op("vector", lambda v: v.tensor_scalar_mul(e_nmy[:], s1[:], -1.0 / C), reads=["fs1"], writes=["fnmy"])
            p.op("vector", lambda v: v.tensor_scalar_mul(e_a[:], ssq[:], 1.0 / C), reads=["fss"], writes=["fa"])
            p.op("vector", lambda v: v.tensor_scalar(e_v[:], e_my[:], e_my[:, 0:1], None, OP.mult, OP.bypass),
                 reads=["fmy"], writes=["fv"])
            p.op("vector", lambda v: v.tensor_sub(e_v[:], e_a[:], e_v[:]), reads=["fa", "fv"], writes=["fv"])
            p.op("vector", lambda v: v.tensor_scalar(e_t[:], e_a[:], 1e-6, 1e-5, OP.add, OP.mult),
                 reads=["fa"], writes=["ft"])
            p.op("vector", lambda v: v.tensor_add(e_d[:], e_v[:], e_t[:]), reads=["fv", "ft"], writes=["fd"])
            p.op("scalar", lambda s: s.activation(e_d[:], e_d[:], AF.Sqrt), reads=["fd"], writes=["fd"])
            p.op("vector", lambda v: v.reciprocal(e_r[:], e_d[:]), reads=["fd"], writes=["fr"])
            p.op("sync", lambda e, row=row: e.dma_start(xt[:], d_x[row:row + P, :]),
                 writes=["xt"], dma="f0xt")
            p.op("vector", lambda v, ya=ya: v.tensor_scalar(otile[:], ya, e_nmy[:, 0:1], e_r[:, 0:1],
                                                            OP.add, OP.mult),
                 reads=["ynat", "fnmy", "fr"], writes=["otile"])
            p.op("vector", lambda v: v.tensor_add(otile[:], otile[:], xt[:]),
                 reads=["otile", "xt"], writes=["otile"])
            p.op("sync", lambda e, row=row: e.dma_start(d_out[row:row + P, :], otile[:]),
                 reads=["otile"], writes=["otile_dma"], dma="ostore")


def prep_f0_weights(gate_up_packed, gate_up_scales, down_packed, down_scales):
    gut = np.ascontiguousarray(gate_up_packed.T)
    g16 = gut.view(np.int16).reshape(256, 11264, 2)
    g16 = np.ascontiguousarray(g16.transpose(2, 0, 1)).reshape(512, 11264)
    gu_src = np.empty((16, 128, 11264), np.int16)
    for b in range(16):
        k = b // 2
        h = k // 4
        j0 = 128 * (b % 2)
        gu_src[b] = g16[256 * h + j0:256 * h + j0 + 128]
    dnt = np.ascontiguousarray(down_packed.T)
    d16 = dnt.view(np.int16).reshape(704, 2048, 2)
    d16 = np.ascontiguousarray(d16.transpose(2, 0, 1)).reshape(1408, 2048)
    dn_src = np.empty((44, 128, 2048), np.int16)
    for m in range(40):
        a, k = m // 8, m % 8
        h = k // 4
        dn_src[m] = d16[704 * h + 128 * a:704 * h + 128 * a + 128]
    for kp in range(4):
        m = 40 + kp
        k1, k2 = 2 * kp, 2 * kp + 1
        dn_src[m, 0:64] = d16[704 * (k1 // 4) + 640:704 * (k1 // 4) + 704]
        dn_src[m, 64:128] = d16[704 * (k2 // 4) + 640:704 * (k2 // 4) + 704]
    sg = np.repeat(gate_up_scales, 16, axis=0)
    scgu = np.stack([sg[0:128], sg[128:256]])
    sd = np.repeat(down_scales, 16, axis=0)
    scdn = np.empty((6, 128, 2048), np.float16)
    for a in range(5):
        scdn[a] = sd[128 * a:128 * (a + 1)]
    scdn[5, 0:64] = sd[640:704]
    scdn[5, 64:128] = sd[640:704]
    return gu_src, dn_src, scgu, scdn


def build_dq():
    nc = bass.Bass("TRN2", target_bir_lowering=False)
    d_gusrc = nc.dram_tensor("gusrc", [16, 128, 11264], I16, kind="ExternalInput")
    d_dnsrc = nc.dram_tensor("dnsrc", [44, 128, 2048], I16, kind="ExternalInput")
    d_scgu = nc.dram_tensor("scgu", [2, 128, 11264], F16, kind="ExternalInput")
    d_scdn = nc.dram_tensor("scdn", [6, 128, 2048], F16, kind="ExternalInput")
    d_wgu = nc.dram_tensor("wgu", [22, 128, 16, 512], BF16, kind="ExternalOutput")
    d_wdn = nc.dram_tensor("wdn", [128, 44, 2048], BF16, kind="ExternalOutput")
    with contextlib.ExitStack() as stack:
        p = Prog(nc, stack)
        emit_dq(p, d_gusrc, d_dnsrc, d_scgu, d_scdn, d_wgu, d_wdn)
        p.finalize()
        p.realize()
    return nc


def build_f0(T=TFULL, TC=512):
    nc = bass.Bass("TRN2", target_bir_lowering=False)
    d_x = nc.dram_tensor("x", [T, C], F32, kind="ExternalInput")
    d_wgu = nc.dram_tensor("wgu", [22, 128, 16, 512], BF16, kind="ExternalInput")
    d_wdn = nc.dram_tensor("wdn", [128, 44, 2048], BF16, kind="ExternalInput")
    d_id = nc.dram_tensor("ident", [128, 128], F32, kind="ExternalInput")
    d_out = nc.dram_tensor("out", [T, C], F32, kind="ExternalOutput")
    with contextlib.ExitStack() as stack:
        p = Prog(nc, stack)
        emit_f0(p, d_x, d_out, d_wgu, d_wdn, d_id, T=T, TC=TC)
        p.finalize()
        p.realize()
    return nc



def make_cvt_pump(p, plan, ncols=2048, nbuf=4, per_call=6):
    bufs = [p.sb(f"cvt_b{i}", [P, ncols], BF16) for i in range(nbuf)]
    pairs = []
    for (src, dst, ntiles, cols) in plan:
        for i in range(ntiles):
            for c0 in range(0, cols, ncols):
                cw = min(ncols, cols - c0)
                if ntiles == 1:
                    pairs.append((src[:, c0:c0 + cw], dst[:, c0:c0 + cw], cw))
                else:
                    pairs.append((src[i, :, c0:c0 + cw], dst[i, :, c0:c0 + cw], cw))
    state = {"i": 0}

    def pump(k=per_call):
        for _ in range(k):
            if state["i"] >= len(pairs):
                return False
            sap, dap, cw = pairs[state["i"]]
            b = state["i"] % nbuf
            state["i"] += 1
            p.op("gpsimd", lambda e, b=b, sap=sap, cw=cw: e.dma_start(bufs[b][:, 0:cw], sap),
                 writes=[f"cvt{b}"], dma=f"cvt{b}")
            p.op("scalar", lambda e, b=b, dap=dap, cw=cw: e.dma_start(dap, bufs[b][:, 0:cw]),
                 reads=[f"cvt{b}"], writes=[f"cvto{b}"], dma=f"cvto{b}")
        return True
    return pump



def emit_a0(p, d_x, d_out, d_wd, d_wlat, d_wp, d_bd, d_blat, d_bp, d_cosT, d_sinT,
            d_ppair, d_md, d_ident, T=TFULL, TQ=1024):
    NH = 16
    DL = 128
    nC = C // P
    ntile = TQ // P
    nq = T // TQ
    SCALE = float(1.0 / np.sqrt(128.0))

    idb, ones = emit_consts(p, d_ident)
    onescol = p.sb("a0_onec", [P, 1], BF16)
    p.op("vector", lambda v: v.memset(onescol[:], 1.0), writes=["onescol"])
    ppair = p.sb("a0_pp", [P, P], BF16)
    mdiag = p.sb("a0_md", [P, P], BF16)
    p.op("gpsimd", lambda g: g.dma_start(ppair[:], d_ppair[:, :]), writes=["ppair"], dma="ppair")
    p.op("gpsimd", lambda g: g.dma_start(mdiag[:], d_md[:, :]), writes=["mdiag"], dma="a0md")
    wlat = p.sb("a0_wlat", [P, C], BF16)
    p.op("gpsimd", lambda g: g.dma_start(wlat[:], d_wlat[:, :]), writes=["wlat"], dma="wlat")
    bd = p.sb("a0_bd", [1, C], BF16)
    blat = p.sb("a0_blat", [1, P], BF16)
    bp = p.sb("a0_bp", [1, C], BF16)
    p.op("gpsimd", lambda g: g.dma_start(bd[:], d_bd[:, :]), writes=["bd"], dma="bd")
    p.op("gpsimd", lambda g: g.dma_start(blat[:], d_blat[:, :]), writes=["blat"], dma="blat")
    p.op("gpsimd", lambda g: g.dma_start(bp[:], d_bp[:, :]), writes=["bp"], dma="bp")

    xt = [p.sb("a0_xt0", [P, C], F32)] * 2
    lt = LNTemps(p, "a0", C)
    lt2 = LNTemps(p, "a0b", C)
    xn = p.sb("a0_xn", [P, C], BF16)
    xnT = p.sb("a0_xnT", [P, nC * TQ], BF16)
    kvnat = p.sb("a0_kvn", [P, T], BF16)
    kvT = p.sb("a0_kvT", [P, TQ], BF16)
    kvrotT = p.sb("a0_kvr", [P, T], BF16)
    cosb = p.sb("a0_cos", [P, TQ], F32)
    sinb = p.sb("a0_sin", [P, TQ], F32)
    rt1 = p.sb("a0_rt1", [P, 512], F32)
    wdq = [p.sb(f"a0_wd{i}", [P, C], BF16) for i in range(2)]
    qT = p.sb("a0_qT", [P, TQ], BF16)
    qrotT = p.sb("a0_qrT", [P, NH * TQ], BF16)
    yT = p.sb("a0_yT", [P, NH * TQ], BF16)
    sq = [p.sb(f"a0_sq{i}", [P, 512], BF16) for i in range(2)]
    ET = [p.sb(f"a0_ET{i}", [P, 512], BF16) for i in range(2)]
    rzf = [p.sb(f"a0_rzf{i}", [1, 512], F32) for i in range(2)]
    rzb = [p.sb(f"a0_rzb{i}", [1, 512], BF16) for i in range(2)]
    rzbc = [p.sb(f"a0_rzc{i}", [P, 512], BF16) for i in range(2)]
    kmax2 = p.sb("a0_km", [1, 1], F32)
    qmax2 = p.sb("a0_qm", [1, 1], F32)
    n11 = p.sb("a0_n11", [1, 1], F32)
    b11 = p.sb("a0_b11", [1, 1], F32)
    negBb = p.sb("a0_nBb", [1, 1], BF16)
    negB = p.sb("a0_nB", [P, 1], F32)
    wpb = p.sb("a0_wp", [P, 16 * 512], BF16)
    otile = p.sb("a0_ot", [P, C], F32)

    pt = p.ps("a0_pt", [P, 1024], BF16)
    pq = [p.ps(f"a0_pq{i}", [P, 512]) for i in range(2)]
    pz = [p.ps(f"a0_pz{i}", [1, 512]) for i in range(2)]
    yAV = [p.ps(f"a0_yav{i}", [P, 512]) for i in range(2)]
    pn = p.ps("a0_pn", [P, 512])

    p.op("vector", lambda v: v.memset(kmax2[:], 0.0), writes=["kmax2"])

    for q in range(nq):
        t0 = q * ntile
        for t in range(ntile):
            row = (t0 + t) * P
            xp = xt[t % 2]
            p.op("sync", lambda e, row=row, xp=xp: e.dma_start(xp[:], d_x[row:row + P, :]),
                 writes=["xt0"], dma="a0xt0")
            emit_ln_stats(p, xp[:], "xt0", lt)
            emit_ln_apply(p, xp[:], "xt0", lt, xn[:], "xn")
            emit_transpose_to_fm(p, lambda b: xn[:, P * b:P * (b + 1)], nC, pt, idb,
                                 xnT, lambda b, t=t: TQ * b + P * t, "xn", "xnT")
        for t in range(ntile):
            g = t0 + t
            pp = pq[t % 2]
            def mkv(tt, t=t, pp=pp):
                ins = None
                for i in range(nC):
                    ins = tt.matmul(pp[:, 0:DL], xnT[:, TQ * i + P * t:TQ * i + P * (t + 1)],
                                    wlat[:, P * i:P * (i + 1)], start=(i == 0), stop=False)
                ins = tt.matmul(pp[:, 0:DL], ones[0:1, 0:P], blat[:], start=False, stop=True)
                return ins
            p.op("tensor", mkv, reads=["xnT", "wlat", "blat", "ones"], writes=[f"pq{t % 2}"])
            p.op("scalar", lambda s, g=g, pp=pp: s.activation(kvnat[:, P * g:P * (g + 1)],
                                                              pp[:, 0:DL], AF.Copy),
                 reads=[f"pq{t % 2}"], writes=["kvnat"])
        p.op("sync", lambda e, q=q: e.dma_start(cosb[:], d_cosT[:, TQ * q:TQ * (q + 1)]),
             writes=["cosb"], dma="a0cos")
        p.op("sync", lambda e, q=q: e.dma_start(sinb[:], d_sinT[:, TQ * q:TQ * (q + 1)]),
             writes=["sinb"], dma="a0sin")
        emit_transpose_to_fm(p, lambda b, t0=t0: kvnat[:, P * (t0 + b):P * (t0 + b + 1)], ntile,
                             pt, idb, kvT, lambda b: P * b, "kvnat", "kvT")
        for cc in range(TQ // 512):
            gsl = slice(TQ * q + 512 * cc, TQ * q + 512 * (cc + 1))
            lsl = slice(512 * cc, 512 * (cc + 1))
            pp = pq[cc % 2]
            p.op("tensor", lambda t_, lsl=lsl, pp=pp: t_.matmul(pp[:], ppair[:], kvT[:, lsl],
                                                                start=True, stop=True),
                 reads=["ppair", "kvT"], writes=[f"pq{cc % 2}"])
            p.op("vector", lambda v, lsl=lsl: v.tensor_mul(rt1[:], kvT[:, lsl], cosb[:, lsl]),
                 reads=["kvT", "cosb"], writes=["rt1"])
            p.op("vector", lambda v, lsl=lsl, pp=pp: v.tensor_mul(pp[:], pp[:], sinb[:, lsl]),
                 reads=[f"pq{cc % 2}", "sinb"], writes=[f"pq{cc % 2}"])
            p.op("vector", lambda v, gsl=gsl, pp=pp: v.tensor_add(kvrotT[:, gsl], rt1[:], pp[:]),
                 reads=["rt1", f"pq{cc % 2}"], writes=["kvrotT"])
            sp = sq[cc % 2]
            p.op("scalar", lambda s, lsl=lsl, sp=sp: s.activation(sp[:], kvT[:, lsl], AF.Square),
                 reads=["kvT"], writes=[f"sq{cc % 2}"])
            p.op("tensor", lambda t_, sp=sp: t_.matmul(pn[0:1, :], onescol[:], sp[:],
                                                       start=True, stop=True),
                 reads=["onescol", f"sq{cc % 2}"], writes=["pn"])
            p.op("vector", lambda v: v.reduce_max(n11[:], pn[0:1, :], axis=AX),
                 reads=["pn"], writes=["n11"])
            p.op("vector", lambda v: v.tensor_tensor(kmax2[:], kmax2[:], n11[:], OP.max),
                 reads=["kmax2", "n11"], writes=["kmax2"])
        p.op("vector", lambda v: v.memset(qmax2[:], 0.0), writes=["qmax2"])
        for h in range(NH):
            par = h % 2
            p.op("gpsimd", lambda g_, h=h, par=par: g_.dma_start(wdq[par][:], d_wd[h, :, :]),
                 writes=[f"wdq{par}"], dma=f"a0wd{par}")
            for cc in range(TQ // 512):
                pp = pq[cc % 2]
                def mq(tt, par=par, cc=cc, h=h, pp=pp):
                    ins = None
                    for i in range(nC):
                        ins = tt.matmul(pp[:], wdq[par][:, P * i:P * (i + 1)],
                                        xnT[:, TQ * i + 512 * cc:TQ * i + 512 * (cc + 1)],
                                        start=(i == 0), stop=False)
                    ins = tt.matmul(pp[:], bd[0:1, P * h:P * (h + 1)], ones[0:1, 0:512],
                                    start=False, stop=True)
                    return ins
                p.op("tensor", mq, reads=[f"wdq{par}", "xnT", "bd", "ones"], writes=[f"pq{cc % 2}"])
                p.op("scalar", lambda s, cc=cc, pp=pp: s.activation(qT[:, 512 * cc:512 * (cc + 1)],
                                                                    pp[:], AF.Copy, scale=SCALE),
                     reads=[f"pq{cc % 2}"], writes=["qT"])
                sp = sq[cc % 2]
                p.op("scalar", lambda s, cc=cc, sp=sp: s.activation(sp[:], qT[:, 512 * cc:512 * (cc + 1)],
                                                                    AF.Square),
                     reads=["qT"], writes=[f"sq{cc % 2}"])
            for cc in range(TQ // 512):
                lsl = slice(512 * cc, 512 * (cc + 1))
                hsl = slice(TQ * h + 512 * cc, TQ * h + 512 * (cc + 1))
                pp = pq[cc % 2]
                p.op("tensor", lambda t_, lsl=lsl, pp=pp: t_.matmul(pp[:], ppair[:], qT[:, lsl],
                                                                    start=True, stop=True),
                     reads=["ppair", "qT"], writes=[f"pq{cc % 2}"])
                p.op("vector", lambda v, lsl=lsl: v.tensor_mul(rt1[:], qT[:, lsl], cosb[:, lsl]),
                     reads=["qT", "cosb"], writes=["rt1"])
                p.op("vector", lambda v, lsl=lsl, pp=pp: v.tensor_mul(pp[:], pp[:], sinb[:, lsl]),
                     reads=[f"pq{cc % 2}", "sinb"], writes=[f"pq{cc % 2}"])
                p.op("vector", lambda v, hsl=hsl, pp=pp: v.tensor_add(qrotT[:, hsl], rt1[:], pp[:]),
                     reads=["rt1", f"pq{cc % 2}"], writes=["qrotT"])
                sp = sq[cc % 2]
                p.op("tensor", lambda t_, sp=sp: t_.matmul(pn[0:1, :], onescol[:], sp[:],
                                                           start=True, stop=True),
                     reads=["onescol", f"sq{cc % 2}"], writes=["pn"])
                p.op("vector", lambda v: v.reduce_max(n11[:], pn[0:1, :], axis=AX),
                     reads=["pn"], writes=["n11"])
                p.op("vector", lambda v: v.tensor_tensor(qmax2[:], qmax2[:], n11[:], OP.max),
                     reads=["qmax2", "n11"], writes=["qmax2"])
        p.op("vector", lambda v: v.tensor_mul(n11[:], qmax2[:], kmax2[:]),
             reads=["qmax2", "kmax2"], writes=["n11"])
        p.op("scalar", lambda s: s.activation(b11[:], n11[:], AF.Sqrt, scale=1.02),
             reads=["n11"], writes=["b11"])
        p.op("vector", lambda v: v.tensor_scalar_mul(negBb[:], b11[:], -1.0),
             reads=["b11"], writes=["negBb"])
        p.op("tensor", lambda t_: t_.matmul(pn[:, 0:1], ones[0:1, 0:P], negBb[:],
                                            start=True, stop=True),
             reads=["ones", "negBb"], writes=["pn"])
        p.op("vector", lambda v: v.tensor_copy(negB[:], pn[:, 0:1]), reads=["pn"], writes=["negB"])
        for h in range(NH):
            for qc in range(TQ // 512):
                nk = 8 * q + 4 * (qc + 1)
                dbase = 8 * q + 4 * qc
                qbase = TQ * h + 512 * qc

                def emitS(k, kp):
                    jmin = max(0, k - dbase)
                    w0 = 128 * jmin
                    pp = pq[kp]
                    p.op("tensor", lambda t_, k=k, qbase=qbase, w0=w0, pp=pp: t_.matmul(
                            pp[:, w0:512], kvrotT[:, P * k:P * (k + 1)],
                            qrotT[:, qbase + w0:qbase + 512],
                            start=True, stop=True),
                         reads=["kvrotT", "qrotT"], writes=[f"pq{kp}"])

                emitS(0, 0)
                for k in range(nk):
                    kp = k % 2
                    jmin = max(0, k - dbase)
                    w0 = 128 * jmin
                    pp = pq[kp]
                    if k + 1 < nk:
                        emitS(k + 1, (k + 1) % 2)
                    p.op("scalar", lambda s, w0=w0, kp=kp, pp=pp: s.activation(
                            ET[kp][:, w0:512], pp[:, w0:512], AF.Exp, bias=negB[:, 0:1]),
                         reads=[f"pq{kp}", "negB"], writes=[f"ET{kp}"])
                    if dbase <= k < dbase + 4:
                        j = k - dbase
                        p.op("vector", lambda v, j=j, kp=kp: v.tensor_mul(
                                ET[kp][:, P * j:P * (j + 1)], ET[kp][:, P * j:P * (j + 1)], mdiag[:]),
                             reads=["mdiag", f"ET{kp}"], writes=[f"ET{kp}"])
                    p.op("tensor", lambda t_, k=k, w0=w0, kp=kp, qc=qc, first=(k == 0), last=(k == nk - 1):
                            t_.matmul(pz[qc][0:1, w0:512], onescol[:], ET[kp][:, w0:512],
                                      start=first, stop=last),
                         reads=["onescol", f"ET{kp}"], writes=[f"pz{qc}"])
                    p.op("tensor", lambda t_, k=k, w0=w0, kp=kp, qc=qc, first=(k == 0), last=(k == nk - 1):
                            t_.matmul(yAV[qc][:, w0:512], kvnat[:, P * k:P * (k + 1)],
                                      ET[kp][:, w0:512], start=first, stop=last),
                         reads=["kvnat", f"ET{kp}"], writes=[f"yav{qc}"])
                p.op("vector", lambda v, qc=qc: v.reciprocal(rzf[qc][:], pz[qc][0:1, :]),
                     reads=[f"pz{qc}"], writes=[f"rzf{qc}"])
                p.op("vector", lambda v, qc=qc: v.tensor_copy(rzb[qc][:], rzf[qc][:]),
                     reads=[f"rzf{qc}"], writes=[f"rzb{qc}"])
            for qc in range(TQ // 512):
                qsl = slice(TQ * h + 512 * qc, TQ * h + 512 * (qc + 1))
                p.op("tensor", lambda t_, qc=qc: t_.matmul(pn[:, :], ones[0:1, 0:P], rzb[qc][:],
                                                           start=True, stop=True),
                     reads=["ones", f"rzb{qc}"], writes=["pn"])
                p.op("vector", lambda v, qc=qc: v.tensor_copy(rzbc[qc][:], pn[:, :]),
                     reads=["pn"], writes=[f"rzc{qc}"])
                p.op("vector", lambda v, qsl=qsl, qc=qc: v.tensor_mul(yT[:, qsl], yAV[qc][:], rzbc[qc][:]),
                     reads=[f"yav{qc}", f"rzc{qc}"], writes=["yT"])
        p.barrier()
        ynat = xnT
        for n in range(4):
            p.op("gpsimd", lambda g_, n=n: g_.dma_start(wpb[:], d_wp[n, :, :]),
                 writes=["wpb"], dma="a0wp")
            for t in range(ntile):
                pp = pq[t % 2]
                def mpr(t_, n=n, t=t, pp=pp):
                    ins = None
                    for k in range(NH):
                        ins = t_.matmul(pp[:], yT[:, TQ * k + P * t:TQ * k + P * (t + 1)],
                                        wpb[:, 512 * k:512 * (k + 1)], start=(k == 0), stop=False)
                    ins = t_.matmul(pp[:], ones[0:1, 0:P], bp[0:1, 512 * n:512 * (n + 1)],
                                    start=False, stop=True)
                    return ins
                p.op("tensor", mpr, reads=["yT", "wpb", "ones", "bp"], writes=[f"pq{t % 2}"])
                p.op("scalar", lambda s, n=n, t=t, pp=pp: s.activation(
                        ynat[:, C * t + 512 * n:C * t + 512 * (n + 1)], pp[:], AF.Copy),
                     reads=[f"pq{t % 2}"], writes=["ynatq"])
        for t in range(ntile):
            row = (t0 + t) * P
            ya = ynat[:, C * t:C * (t + 1)]
            xp = xt[t % 2]
            emit_ln_stats(p, ya, "ynatq", lt2)
            emit_ln_apply(p, ya, "ynatq", lt2, otile[:], "otile")
            p.op("sync", lambda e, row=row, xp=xp: e.dma_start(xp[:], d_x[row:row + P, :]),
                 writes=["xt0"], dma="a0xt0")
            p.op("vector", lambda v, xp=xp: v.tensor_add(otile[:], otile[:], xp[:]),
                 reads=["otile", "xt0"], writes=["otile"])
            p.op("sync", lambda e, row=row: e.dma_start(d_out[row:row + P, :], otile[:]),
                 reads=["otile"], writes=["otile_dma"], dma="ostore")
        p.barrier()


def prep_a0_weights(W_latent, b_latent, W_d, b_d, W_proj, b_proj):
    wd = np.ascontiguousarray(W_d.reshape(16, 128, 16, 128).transpose(2, 1, 0, 3)).reshape(16, 128, 2048)
    wlat = np.ascontiguousarray(W_latent.reshape(16, 128, 128).transpose(1, 0, 2)).reshape(128, 2048)
    wp = np.ascontiguousarray(W_proj.reshape(16, 128, 4, 512).transpose(2, 1, 0, 3)).reshape(4, 128, 8192)
    return (wd, wlat, wp, b_d.reshape(1, 2048), b_latent.reshape(1, 128), b_proj.reshape(1, 2048))


def a0_consts(T=TFULL):
    half = 64
    pos = np.arange(T, dtype=np.float64)
    inv = 10000.0 ** (-np.arange(half, dtype=np.float64) / half)
    ang = pos[:, None] * inv[None, :]
    cosT = np.repeat(np.cos(ang).T, 2, axis=0).astype(np.float32)
    sinT = np.repeat(np.sin(ang).T, 2, axis=0).astype(np.float32)
    sinT[0::2, :] *= -1.0
    pp = np.zeros((128, 128), np.float32)
    for k in range(64):
        pp[2 * k, 2 * k + 1] = 1.0
        pp[2 * k + 1, 2 * k] = 1.0
    r = np.arange(128)
    mtri = np.where(r[:, None] >= r[None, :], 0.0, -1e9).astype(np.float32)
    return cosT, sinT, pp, mtri


def build_a0(T=TFULL, TQ=1024):
    nc = bass.Bass("TRN2", target_bir_lowering=False)
    d_x = nc.dram_tensor("x", [T, C], F32, kind="ExternalInput")
    d_wd = nc.dram_tensor("wd", [16, 128, 2048], F32, kind="ExternalInput")
    d_wlat = nc.dram_tensor("wlat", [128, 2048], F32, kind="ExternalInput")
    d_wp = nc.dram_tensor("wp", [4, 128, 8192], F32, kind="ExternalInput")
    d_bd = nc.dram_tensor("bd", [1, 2048], F32, kind="ExternalInput")
    d_blat = nc.dram_tensor("blat", [1, 128], F32, kind="ExternalInput")
    d_bp = nc.dram_tensor("bp", [1, 2048], F32, kind="ExternalInput")
    d_cosT = nc.dram_tensor("cosT", [128, T], F32, kind="ExternalInput")
    d_sinT = nc.dram_tensor("sinT", [128, T], F32, kind="ExternalInput")
    d_ppair = nc.dram_tensor("ppair", [128, 128], F32, kind="ExternalInput")
    d_mtri = nc.dram_tensor("mtri", [128, 128], F32, kind="ExternalInput")
    d_id = nc.dram_tensor("ident", [128, 128], F32, kind="ExternalInput")
    d_out = nc.dram_tensor("out", [T, C], F32, kind="ExternalOutput")
    with contextlib.ExitStack() as stack:
        p = Prog(nc, stack)
        emit_a0(p, d_x, d_out, d_wd, d_wlat, d_wp, d_bd, d_blat, d_bp, d_cosT, d_sinT,
                d_ppair, d_mtri, d_id, T=T, TQ=TQ)
        p.finalize()
        p.realize()
    return nc


def build_full(T=TFULL):
    nc = bass.Bass("TRN2", target_bir_lowering=False)
    d_x = nc.dram_tensor("x", [T, C], F32, kind="ExternalInput")
    d_gusrc = nc.dram_tensor("gusrc", [16, 128, 11264], I16, kind="ExternalInput")
    d_dnsrc = nc.dram_tensor("dnsrc", [44, 128, 2048], I16, kind="ExternalInput")
    d_scgu = nc.dram_tensor("scgu", [2, 128, 11264], F16, kind="ExternalInput")
    d_scdn = nc.dram_tensor("scdn", [6, 128, 2048], F16, kind="ExternalInput")
    d_wd = nc.dram_tensor("wd", [16, 128, 2048], F32, kind="ExternalInput")
    d_wlat = nc.dram_tensor("wlat", [128, 2048], F32, kind="ExternalInput")
    d_wp0 = nc.dram_tensor("wp0", [4, 128, 8192], F32, kind="ExternalInput")
    d_bd = nc.dram_tensor("bd", [1, 2048], F32, kind="ExternalInput")
    d_blat = nc.dram_tensor("blat", [1, 128], F32, kind="ExternalInput")
    d_bp = nc.dram_tensor("bp", [1, 2048], F32, kind="ExternalInput")
    d_cosT = nc.dram_tensor("cosT", [128, T], F32, kind="ExternalInput")
    d_sinT = nc.dram_tensor("sinT", [128, T], F32, kind="ExternalInput")
    d_ppair = nc.dram_tensor("ppair", [128, 128], F32, kind="ExternalInput")
    d_mtri = nc.dram_tensor("mtri", [128, 128], F32, kind="ExternalInput")
    d_wq1 = nc.dram_tensor("wq1", [12, 128, 4096], F32, kind="ExternalInput")
    d_wp1 = nc.dram_tensor("wp1", [4, 128, 8192], F32, kind="ExternalInput")
    d_md = nc.dram_tensor("mdiag", [128, 128], F32, kind="ExternalInput")
    d_ma = nc.dram_tensor("manti", [128, 128], F32, kind="ExternalInput")
    d_wgu1 = nc.dram_tensor("wgu1", [64, 128, 4096], F32, kind="ExternalInput")
    d_wdn1 = nc.dram_tensor("wdn1", [16, 128, 8192], F32, kind="ExternalInput")
    d_id = nc.dram_tensor("ident", [128, 128], F32, kind="ExternalInput")
    d_out = nc.dram_tensor("out", [T, C], F32, kind="ExternalOutput")
    d_wgu = nc.dram_tensor("i_wgu", [22, 128, 16, 512], BF16, kind="Internal")
    d_wdnq = nc.dram_tensor("i_wdn", [128, 44, 2048], BF16, kind="Internal")
    d_x1 = nc.dram_tensor("i_x1", [T, C], F32, kind="Internal")
    d_x2 = nc.dram_tensor("i_x2", [T, C], F32, kind="Internal")
    d_x3 = nc.dram_tensor("i_x3", [T, C], F32, kind="Internal")
    d_wd_b = nc.dram_tensor("b_wd", [16, 128, 2048], BF16, kind="Internal")
    d_wp0_b = nc.dram_tensor("b_wp0", [4, 128, 8192], BF16, kind="Internal")
    d_wq1_b = nc.dram_tensor("b_wq1", [12, 128, 4096], BF16, kind="Internal")
    d_wp1_b = nc.dram_tensor("b_wp1", [4, 128, 8192], BF16, kind="Internal")
    d_wgu1_b = nc.dram_tensor("b_wgu1", [64, 128, 4096], BF16, kind="Internal")
    d_wdn1_b = nc.dram_tensor("b_wdn1", [16, 128, 8192], BF16, kind="Internal")
    cvt_plan = [
        (d_wd, d_wd_b, 16, 2048), (d_wp0, d_wp0_b, 4, 8192),
        (d_wq1, d_wq1_b, 12, 4096), (d_wp1, d_wp1_b, 4, 8192),
        (d_wgu1, d_wgu1_b, 64, 4096),
        (d_wdn1, d_wdn1_b, 16, 8192),
    ]
    with contextlib.ExitStack() as stack:
        p = Prog(nc, stack)
        with contextlib.ExitStack() as st:
            p.new_scope(st)
            pump = make_cvt_pump(p, cvt_plan)
            emit_dq(p, d_gusrc, d_dnsrc, d_scgu, d_scdn, d_wgu, d_wdnq, pump=pump)
            while pump(50):
                pass
            p.barrier()
        with contextlib.ExitStack() as st:
            p.new_scope(st)
            emit_a0(p, d_x, d_x1, d_wd_b, d_wlat, d_wp0_b, d_bd, d_blat, d_bp, d_cosT, d_sinT,
                    d_ppair, d_md, d_id, T=T)
            p.barrier()
        with contextlib.ExitStack() as st:
            p.new_scope(st)
            emit_f0(p, d_x1, d_x2, d_wgu, d_wdnq, d_id, T=T)
            p.barrier()
        with contextlib.ExitStack() as st:
            p.new_scope(st)
            emit_a1(p, d_x2, d_x3, d_wq1_b, d_wp1_b, d_cosT, d_sinT, d_md, d_ma, d_ppair, d_id, T=T)
            p.barrier()
        with contextlib.ExitStack() as st:
            p.new_scope(st)
            emit_f1(p, d_x3, d_out, d_wgu1_b, d_wdn1_b, d_id, T=T)
            p.barrier()
        p.finalize()
        p.realize()
    return nc


_BUILT = {}


def _get(name, builder):
    if name not in _BUILT:
        _BUILT[name] = builder()
    return _BUILT[name]


def kernel(L0_attn_x, L0_attn_W_latent, L0_attn_b_latent, L0_attn_W_d, L0_attn_b_d,
           L0_attn_W_proj, L0_attn_b_proj, L0_ffn_gate_up_packed, L0_ffn_gate_up_scales,
           L0_ffn_down_packed, L0_ffn_down_scales, L1_attn_w_qkv, L1_attn_w_proj,
           L1_ffn_w_gate, L1_ffn_w_up, L1_ffn_w_down):
    cores = list(range(NCORES))
    B = L0_attn_x.shape[0]
    assert B == NCORES

    gu_src, dn_src, scgu, scdn = prep_f0_weights(
        L0_ffn_gate_up_packed, L0_ffn_gate_up_scales, L0_ffn_down_packed, L0_ffn_down_scales)
    a0w = prep_a0_weights(L0_attn_W_latent, L0_attn_b_latent, L0_attn_W_d, L0_attn_b_d,
                          L0_attn_W_proj, L0_attn_b_proj)
    wd0, wlat0, wp0, bd0, blat0, bp0 = a0w
    cosT, sinT, ppair, mtri = a0_consts(T=TFULL)
    wq1, wp1 = prep_a1_weights(L1_attn_w_qkv, L1_attn_w_proj)
    md1, ma1 = a1_masks()
    wgu1, wdn1 = prep_f1_weights(L1_ffn_w_gate, L1_ffn_w_up, L1_ffn_w_down)

    nc = _get("full", build_full)
    maps = [{"x": np.ascontiguousarray(L0_attn_x[c]),
             "gusrc": gu_src, "dnsrc": dn_src, "scgu": scgu, "scdn": scdn,
             "wd": wd0, "wlat": wlat0, "wp0": wp0, "bd": bd0, "blat": blat0, "bp": bp0,
             "cosT": cosT, "sinT": sinT, "ppair": ppair, "mtri": mtri,
             "wq1": wq1, "wp1": wp1, "mdiag": md1, "manti": ma1,
             "wgu1": wgu1, "wdn1": wdn1, "ident": IDENT} for c in cores]
    res = run_bass_kernel_spmd(nc, maps, core_ids=cores).results
    return np.stack([np.asarray(res[c]["out"], dtype=np.float32) for c in cores])
